# version 03
# baseline (speedup 1.0000x reference)
import contextlib
import numpy as np

import concourse.bass as bass
import concourse.mybir as mybir
from concourse.bass_utils import run_bass_kernel_spmd

F32 = mybir.dt.float32
BF16 = mybir.dt.bfloat16
AF = mybir.ActivationFunctionType
ALU = mybir.AluOpType
AX = mybir.AxisListType

B = 2
S = 2048
D = 2048
HD = 128
NH = 16
NKV = 4
DKV = NKV * HD
HFF = 8192
TOK = B * S
NKT = S // 128
QT = 512
TFF = 512

LN_EPS = 1e-5
RMS_EPS = 1.1920929e-07
ISQ = 0.08838834764831845
QTHR = 0.5009765625
MAGIC = 12582912.0
TWO_PI_HI = 6.28125
TWO_PI_LO = 0.0019353071795864769
INV_2PI = 0.15915494309189535
LN1E4 = 9.210340371976184
HALF_PI = 1.5707963267948966

ENGINES = ("TE", "ACT", "DVE", "POOL", "SP")
_BLOCK = dict(TE="tensor", ACT="scalar", DVE="vector", POOL="gpsimd", SP="sync")
TE, ACT, DVE, POOL, SP = ENGINES


class Res(object):
    __slots__ = ("w", "r", "burst")

    def __init__(self):
        self.w = None
        self.r = []
        self.burst = set()


NDSLOT = 12


class Plan(object):
    def __init__(self):
        self.ops = []
        self.eng = {e: [] for e in ENGINES}
        self.last_dma = {}
        self.fence = {e: -1 for e in ENGINES}

    def op(self, eng, emit, reads=(), writes=(), dma=False, deps=(), tag="",
           dslot=0, join=False):
        i = len(self.ops)
        dep = set(d for d in deps if d is not None)
        for res in reads:
            if res.w is not None:
                dep.add(res.w)
        for res in writes:
            if join:
                dep.update(res.burst)
            else:
                dep.update(res.r)
                if res.w is not None:
                    dep.add(res.w)
        rec = {"i": i, "eng": eng, "emit": emit, "deps": dep, "dma": dma,
               "inc": False, "pos": len(self.eng[eng]), "tag": tag,
               "slot": dslot if dma else None, "fence": self.fence[eng]}
        self.ops.append(rec)
        self.eng[eng].append(rec)
        if dma:
            self.last_dma[(eng, dslot)] = i
        for res in reads:
            res.r.append(i)
        for res in writes:
            if not join:
                res.burst = set(res.r)
                if res.w is not None:
                    res.burst.add(res.w)
                res.r = []
            res.w = i
        return i

    def barrier(self, tag="barrier"):
        ids = []
        for e in ENGINES:
            if self.eng[e]:
                ids.append(self.eng[e][-1]["i"])
        ids.extend(self.last_dma.values())
        ids = sorted(set(ids))
        bstart = len(self.ops)
        for e in ENGINES:
            self.op(e, lambda eo: eo.nop(), deps=ids, tag=tag)
        for e in ENGINES:
            self.fence[e] = bstart

    def lower(self, nc, block, csem, dsem):
        for op in self.ops:
            best = {}
            for d in op["deps"]:
                if d < op["fence"]:
                    continue
                p = self.ops[d]
                key = (p["eng"], p["dma"], p["slot"])
                q = best.get(key)
                if q is None or p["pos"] > q["pos"]:
                    best[key] = p
            wl = []
            for (peng, pdma, pslot_), p in best.items():
                if peng == op["eng"] and not pdma and peng == TE:
                    continue
                p["inc"] = True
                wl.append(p)
            op["waits"] = wl
        for e in ENGINES:
            c = 0
            dc = [0] * NDSLOT
            for op in self.eng[e]:
                if op["dma"]:
                    s = op["slot"]
                    dc[s] += 16
                    op["val"] = dc[s]
                    op["sem"] = dsem[e][s]
                elif op["inc"]:
                    c += 1
                    op["val"] = c
                    op["sem"] = csem[e]

        def make(eng):
            def body(eo):
                hwm = {}
                for op in self.eng[eng]:
                    for p in op["waits"]:
                        s = p["sem"]
                        v = p["val"]
                        k = id(s)
                        if hwm.get(k, 0) < v:
                            eo.wait_ge(s, v)
                            hwm[k] = v
                    ins = op["emit"](eo)
                    if op["dma"]:
                        ins.then_inc(op["sem"], 16)
                    elif op["inc"]:
                        ins.then_inc(op["sem"], 1)
            return body
        for e in ENGINES:
            getattr(block, _BLOCK[e])(make(e))


def build_program(dbg=()):
    nc = bass.Bass("TRN2", target_bir_lowering=False)

    x_in = nc.dram_tensor("x", [B, S, D], F32, kind="ExternalInput").ap()
    wq = nc.dram_tensor("wq", [D, D], F32, kind="ExternalInput").ap()
    wk = nc.dram_tensor("wk", [DKV, D], F32, kind="ExternalInput").ap()
    wv = nc.dram_tensor("wv", [DKV, D], F32, kind="ExternalInput").ap()
    wo = nc.dram_tensor("wo", [D, D], F32, kind="ExternalInput").ap()
    wgu = nc.dram_tensor("wgu", [2 * HFF, D], F32, kind="ExternalInput").ap()
    wp = nc.dram_tensor("wp", [D, HFF], F32, kind="ExternalInput").ap()
    y_out = nc.dram_tensor("y", [B, S, D], F32, kind="ExternalOutput").ap()

    wq_bf = nc.dram_tensor("wq_bf", [D, D], BF16, kind="Internal").ap()
    wk_bf = nc.dram_tensor("wk_bf", [DKV, D], BF16, kind="Internal").ap()
    wv_bf = nc.dram_tensor("wv_bf", [DKV, D], BF16, kind="Internal").ap()
    wo_bf = nc.dram_tensor("wo_bf", [D, D], BF16, kind="Internal").ap()
    wgu_h = nc.dram_tensor("wgu_h", [2 * HFF, D], BF16, kind="Internal").ap()
    wp_h = nc.dram_tensor("wp_h", [D, HFF], BF16, kind="Internal").ap()
    x1scr = nc.dram_tensor("x1scr", [TOK, D], F32, kind="Internal").ap()

    dbg_out = {}
    for name, shape, dt in (
        ("d_cos", [128, NKT * 64], F32), ("d_sin", [128, NKT * 64], F32),
        ("d_kfm", [128, NKV * S], F32), ("d_vtm", [128, NKT * DKV], F32),
        ("d_qfm", [128, NH * QT], F32), ("d_ofm", [128, NH * QT], F32),
        ("d_x1", [TOK, D], F32),
        ("d_wgu", [2 * HFF, D], BF16), ("d_wp", [D, HFF], BF16),
        ("d_hfm", [128, 64 * TFF], F32),
    ):
        if name in dbg:
            dbg_out[name] = nc.dram_tensor(name, shape, dt,
                                           kind="ExternalOutput").ap()

    P = Plan()

    r_ident = Res()
    r_identf = Res()
    r_ones = Res()
    r_cns = Res()
    r_sqscr = Res()
    r_wqbf, r_wkbf, r_wvbf, r_wobf = Res(), Res(), Res(), Res()
    r_wguh = Res()
    r_wph = Res()
    r_x1 = [Res() for _ in range(TOK // 128)]
    r_cos, r_sin, r_tmp = Res(), Res(), Res()
    r_kfm, r_vtm, r_qfm, r_ofm = Res(), Res(), Res(), Res()
    r_xb = [Res(), Res()]
    r_ab = [Res(), Res()]
    r_afm = [Res() for _ in range(4)]
    r_qb = [Res() for _ in range(4)]
    r_kr = [Res(), Res()]
    r_ex = [Res(), Res()]
    r_da, r_ds, r_rd = Res(), Res(), Res()
    r_wr = [Res(), Res(), Res()]
    r_fs = [Res() for _ in range(4)]
    r_st = [Res() for _ in range(16)]
    r_x1st = Res()
    r_x1o = Res()
    r_qt = Res()
    r_gur = [Res(), Res(), Res()]
    r_wpr = [Res(), Res(), Res()]
    r_hfm = Res()
    r_ffm = Res()
    r_ftmp = Res()
    r_ffsb = Res()
    r_sqb = [Res(), Res()]
    r_rst = Res()
    bankres = [Res() for _ in range(8)]

    with contextlib.ExitStack() as st:
        def sb(name, shape, dt):
            return st.enter_context(nc.sbuf_tensor(name, shape, dt))

        ident = sb("ident", [128, 128], BF16)
        identf = sb("identf", [128, 128], F32)
        ones_col = sb("ones_col", [128, 1], F32)
        ones_row = sb("ones_row", [1, 128], F32)
        ones_bcol = sb("ones_bcol", [128, 1], BF16)
        xbuf = sb("xbuf", [128, 2 * D], F32)
        abf = sb("abf", [128, 2 * D], BF16)
        sqscr = sb("sqscr", [128, D], BF16)
        stt = sb("stt", [128, 128], F32)
        x1st = sb("x1st", [128, D], F32)
        cns = sb("cns", [128, 3], F32)
        pA = st.enter_context(nc.psum_tensor("pA", [128, 2048], F32))
        pB = st.enter_context(nc.psum_tensor("pB", [128, 2048], F32))
        csem = {e: st.enter_context(nc.semaphore("c" + e)) for e in ENGINES}
        dsem = {e: [st.enter_context(nc.semaphore("d%s%d" % (e, s)))
                    for s in range(NDSLOT)]
                for e in (SP, POOL)}
        dsem[TE] = dsem[ACT] = dsem[DVE] = dsem[SP]

        def pslot(k):
            t = pA if k < 2 else pB
            o = (k % 2) * 1024
            return (t[:, o:o + 1024],
                    t[:].bitcast(BF16)[:, 2 * o:2 * o + 2048],
                    [bankres[2 * k], bankres[2 * k + 1]])

        def pbank(k):
            t = pA if k < 4 else pB
            o = (k % 4) * 512
            return t[:, o:o + 512], [bankres[k]]

        P.op(POOL, lambda eo: eo.memset(ident[:], 0.0), writes=[r_ident])
        P.op(POOL, lambda eo: eo.affine_select(
            out=ident[:], in_=ident[:], compare_op=ALU.not_equal, fill=1.0,
            base=0, pattern=[[-1, 128]], channel_multiplier=1),
            reads=[r_ident], writes=[r_ident])
        P.op(POOL, lambda eo: eo.memset(identf[:], 0.0), writes=[r_identf])
        P.op(POOL, lambda eo: eo.affine_select(
            out=identf[:], in_=identf[:], compare_op=ALU.not_equal, fill=1.0,
            base=0, pattern=[[-1, 128]], channel_multiplier=1),
            reads=[r_identf], writes=[r_identf])
        P.op(POOL, lambda eo: eo.memset(ones_col[:], 1.0), writes=[r_ones])
        P.op(POOL, lambda eo: eo.memset(ones_row[:], 1.0), writes=[r_ones])
        P.op(POOL, lambda eo: eo.memset(ones_bcol[:], 1.0), writes=[r_ones])
        P.op(POOL, lambda eo: eo.memset(cns[:, 0:1], HALF_PI), writes=[r_cns])
        P.op(POOL, lambda eo: eo.memset(cns[:, 1:2], LN_EPS), writes=[r_cns])
        P.op(POOL, lambda eo: eo.memset(cns[:, 2:3], RMS_EPS), writes=[r_cns])

        P.op(POOL, lambda eo: eo.dma_start(wq_bf, wq), writes=[r_wqbf],
             dma=True, dslot=0)
        P.op(POOL, lambda eo: eo.dma_start(wk_bf, wk), writes=[r_wkbf],
             dma=True, dslot=1)
        P.op(POOL, lambda eo: eo.dma_start(wv_bf, wv), writes=[r_wvbf],
             dma=True, dslot=2)
        P.op(POOL, lambda eo: eo.dma_start(wo_bf, wo), writes=[r_wobf],
             dma=True, dslot=3)

        def ln_stats(src_ap, src_deps, col, nelem=D):
            su = stt[:, col + 0:col + 1]
            sq = stt[:, col + 1:col + 2]
            m = stt[:, col + 2:col + 3]
            ms = stt[:, col + 3:col + 4]
            v = stt[:, col + 4:col + 5]
            v2 = stt[:, col + 5:col + 6]
            rs = stt[:, col + 6:col + 7]
            nb = stt[:, col + 7:col + 8]
            rstat = r_st[col // 8]
            P.op(DVE, lambda eo: eo.tensor_reduce(su, src_ap, AX.X, ALU.add),
                 reads=src_deps, writes=[rstat])
            P.op(ACT, lambda eo: eo.activation(
                sqscr[:, 0:nelem], src_ap, AF.Square, accum_out=sq),
                reads=src_deps, writes=[rstat, r_sqscr])
            P.op(DVE, lambda eo: eo.tensor_scalar(m, su, 1.0 / nelem, None,
                                                  ALU.mult),
                 reads=[rstat], writes=[rstat])
            P.op(DVE, lambda eo: eo.tensor_scalar(ms, m, m, None, ALU.mult),
                 reads=[rstat], writes=[rstat])
            P.op(DVE, lambda eo: eo.tensor_scalar(v, sq, 1.0 / nelem, None,
                                                  ALU.mult),
                 reads=[rstat], writes=[rstat])
            P.op(DVE, lambda eo: eo.tensor_tensor(v2, v, ms, ALU.subtract),
                 reads=[rstat], writes=[rstat])
            P.op(ACT, lambda eo: eo.activation(ms, v2, AF.Ln, bias=cns[:, 1:2]),
                 reads=[rstat, r_cns], writes=[rstat])
            P.op(ACT, lambda eo: eo.activation(rs, ms, AF.Exp, scale=-0.5),
                 reads=[rstat], writes=[rstat])
            P.op(DVE, lambda eo: eo.tensor_scalar(nb, m, rs, -1.0,
                                                  ALU.mult, ALU.mult),
                 reads=[rstat], writes=[rstat])
            return rs, nb, rstat

        def ln_apply(src_ap, src_deps, out_ap, out_res, rs, nb, rstat):
            P.op(ACT, lambda eo: eo.activation(out_ap, src_ap, AF.Identity,
                                               bias=nb, scale=rs),
                 reads=list(src_deps) + [rstat], writes=[out_res])

        def transpose16(src_bf, src_deps, dst_ap, dst_res, slot, nblk=16):
            ps_f, ps_b, banks = pslot(slot)
            for c in range(nblk):
                P.op(TE, lambda eo, c=c: eo.transpose(
                    ps_b[:, c * 128:(c + 1) * 128],
                    src_bf[:, c * 128:(c + 1) * 128], ident[:]),
                    reads=list(src_deps) + [r_ident], writes=banks)
            P.op(DVE, lambda eo: eo.tensor_copy(dst_ap, ps_b[:, 0:nblk * 128]),
                 reads=banks, writes=[dst_res])

        with contextlib.ExitStack() as sa:
            def sba(name, shape, dt):
                return sa.enter_context(nc.sbuf_tensor(name, shape, dt))

            cosb = sba("cosb", [128, NKT * 64], F32)
            sinb = sba("sinb", [128, NKT * 64], F32)
            kfm = sba("kfm", [128, NKV * S], BF16)
            vtm = sba("vtm", [128, NKT * DKV], BF16)
            qfm = sba("qfm", [128, NH * QT], BF16)
            ofm = sba("ofm", [128, NH * QT], BF16)
            afm = sba("afm", [128, 4 * D], BF16)
            qbf = sba("qbf", [128, 4 * D], BF16)
            krop = sba("krop", [128, 1 * DKV], BF16)
            rpt = sba("rpt", [128, 2 * 512], F32)
            expb = sba("expb", [128, 2 * QT], BF16)
            dacc = sba("dacc", [128, QT], F32)
            dsb = sba("dsb", [1, QT], F32)
            rden = sba("rden", [128, QT], F32)
            wring = sba("wring", [128, 3 * D], BF16)
            fstage = sba("fstage", [128, 4 * 1024], F32)
            x1o = sba("x1o", [128, D], F32)
            qwb = sba("qwb", [128, D], BF16)
            qaw = sba("qaw", [128, D], BF16)
            qm = sba("qm", [128, D], BF16)
            qs = sba("qs", [128, D], BF16)
            qsm = sba("qsm", [128, 64], F32)
            qscale = sba("qscale", [128, 32], BF16)

            jg = rpt[:, 0:64]
            invf = rpt[:, 64:128]
            fq = rpt[:, 128:192]
            tq = rpt[:, 192:256]
            tq2 = rpt[:, 256:320]
            tr = rpt[:, 320:384]
            tr2 = rpt[:, 384:448]
            pc = rpt[:, 448:449]
            P.op(POOL, lambda eo: eo.iota(jg, pattern=[[1, 64]], base=0,
                                          channel_multiplier=0,
                                          allow_small_or_imprecise_dtypes=True),
                 writes=[r_tmp])
            P.op(ACT, lambda eo: eo.activation(invf, jg, AF.Exp,
                                               scale=-LN1E4 / 64.0),
                 reads=[r_tmp], writes=[r_tmp])
            for kt in range(NKT):
                P.op(POOL, lambda eo, kt=kt: eo.iota(
                    pc, pattern=[[0, 1]], base=kt * 128, channel_multiplier=1,
                    allow_small_or_imprecise_dtypes=True),
                    reads=[r_tmp], writes=[r_tmp])
                P.op(DVE, lambda eo: eo.tensor_scalar(fq, invf, pc, None,
                                                      ALU.mult),
                     reads=[r_tmp], writes=[r_tmp])
                for dstap, extra in ((sinb, 0.0), (cosb, 0.25)):
                    if extra:
                        P.op(DVE, lambda eo: eo.tensor_scalar(
                            tq, fq, INV_2PI, 0.25, ALU.mult, ALU.add),
                            reads=[r_tmp], writes=[r_tmp])
                        P.op(DVE, lambda eo: eo.tensor_scalar(
                            tq, tq, MAGIC, None, ALU.add),
                            reads=[r_tmp], writes=[r_tmp])
                    else:
                        P.op(DVE, lambda eo: eo.tensor_scalar(
                            tq, fq, INV_2PI, MAGIC, ALU.mult, ALU.add),
                            reads=[r_tmp], writes=[r_tmp])
                    P.op(DVE, lambda eo: eo.tensor_scalar(
                        tq2, tq, MAGIC, None, ALU.subtract),
                        reads=[r_tmp], writes=[r_tmp])
                    P.op(DVE, lambda eo: eo.scalar_tensor_tensor(
                        tr, tq2, -TWO_PI_HI, fq, ALU.mult, ALU.add),
                        reads=[r_tmp], writes=[r_tmp])
                    P.op(DVE, lambda eo: eo.scalar_tensor_tensor(
                        tr2, tq2, -TWO_PI_LO, tr, ALU.mult, ALU.add),
                        reads=[r_tmp], writes=[r_tmp])
                    if extra:
                        P.op(DVE, lambda eo: eo.tensor_scalar(
                            tr2, tr2, -4.7123885, 1.5707962, ALU.max, ALU.min),
                            reads=[r_tmp], writes=[r_tmp])
                        P.op(ACT, lambda eo, kt=kt, dstap=dstap:
                             eo.activation(dstap[:, kt * 64:(kt + 1) * 64],
                                           tr2, AF.Sin, bias=cns[:, 0:1]),
                             reads=[r_tmp, r_cns], writes=[r_cos])
                    else:
                        P.op(DVE, lambda eo: eo.tensor_scalar(
                            tr2, tr2, -3.1415923, 3.1415923, ALU.max, ALU.min),
                            reads=[r_tmp], writes=[r_tmp])
                        P.op(ACT, lambda eo, kt=kt, dstap=dstap:
                             eo.activation(dstap[:, kt * 64:(kt + 1) * 64],
                                           tr2, AF.Sin),
                             reads=[r_tmp], writes=[r_sin])

            if "d_cos" in dbg_out:
                P.op(SP, lambda eo: eo.dma_start(dbg_out["d_cos"], cosb[:]),
                     reads=[r_cos], dma=True)
                P.op(SP, lambda eo: eo.dma_start(dbg_out["d_sin"], sinb[:]),
                     reads=[r_sin], dma=True)

            quant_jobs = []
            for i in range(2 * HFF // 128):
                quant_jobs.append(("gu", i, 0))
            for i in range(D // 128):
                for c in range(HFF // D):
                    quant_jobs.append(("wp", i, c))
            gsum = qsm[:, 0:32]
            thr = qsm[:, 32:64]

            def emit_quant(job):
                kind, i, c = job
                if kind == "gu":
                    src = wgu[i * 128:(i + 1) * 128, :]
                    dst = wgu_h[i * 128:(i + 1) * 128, :]
                    r_dst = r_wguh
                else:
                    src = wp[i * 128:(i + 1) * 128, c * D:(c + 1) * D]
                    dst = wp_h[i * 128:(i + 1) * 128, c * D:(c + 1) * D]
                    r_dst = r_wph
                P.op(POOL, lambda eo, src=src: eo.dma_start(qwb[:], src),
                     writes=[r_qt], dma=True, tag="qload", dslot=4)
                P.op(ACT, lambda eo: eo.activation(qaw[:], qwb[:], AF.Abs),
                     reads=[r_qt], writes=[r_qt])
                P.op(ACT, lambda eo: eo.activation(qs[:], qwb[:], AF.Sign),
                     reads=[r_qt], writes=[r_qt])
                P.op(DVE, lambda eo: eo.tensor_reduce(
                    gsum, qaw[:].rearrange("p (g e) -> p g e", e=64),
                    AX.X, ALU.add),
                    reads=[r_qt], writes=[r_qt])
                P.op(DVE, lambda eo: eo.tensor_scalar(
                    qscale[:], gsum, 1.0 / 64.0, 1e-8, ALU.mult, ALU.max),
                    reads=[r_qt], writes=[r_qt])
                P.op(DVE, lambda eo: eo.tensor_scalar(
                    thr, qscale[:], QTHR, None, ALU.mult),
                    reads=[r_qt], writes=[r_qt])
                P.op(DVE, lambda eo: eo.tensor_tensor(
                    qm[:].rearrange("p (g e) -> p g e", e=64),
                    qaw[:].rearrange("p (g e) -> p g e", e=64),
                    thr.rearrange("p (g e) -> p g e", e=1)
                       .broadcast_to([128, 32, 64]),
                    ALU.is_gt),
                    reads=[r_qt], writes=[r_qt])
                P.op(DVE, lambda eo: eo.tensor_tensor(qaw[:], qm[:], qs[:],
                                                      ALU.mult),
                     reads=[r_qt], writes=[r_qt])
                P.op(DVE, lambda eo: eo.tensor_tensor(
                    qm[:].rearrange("p (g e) -> p g e", e=64),
                    qaw[:].rearrange("p (g e) -> p g e", e=64),
                    qscale[:].rearrange("p (g e) -> p g e", e=1)
                        .broadcast_to([128, 32, 64]),
                    ALU.mult),
                    reads=[r_qt], writes=[r_qt])
                P.op(SP, lambda eo, dst=dst: eo.dma_start(dst, qm[:]),
                     reads=[r_qt], writes=[r_dst], dma=True, tag="qstore",
                     dslot=5, join=True)

            def pump_quant(n=1):
                for _ in range(n):
                    if quant_jobs:
                        emit_quant(quant_jobs.pop(0))

            def rope6(src_f32_1024, src_deps, nh8, kti, dst_bf_1024, dst_res):
                e = 64
                sv = src_f32_1024.rearrange("p (h q) -> p h q", q=128)
                k1 = sv[:, :, 0:e]
                k2 = sv[:, :, e:128]
                cb = cosb[:, kti * 64:(kti + 1) * 64].unsqueeze(1) \
                    .broadcast_to([128, nh8, e])
                sbn = sinb[:, kti * 64:(kti + 1) * 64].unsqueeze(1) \
                    .broadcast_to([128, nh8, e])
                t1 = rpt[:, 0:nh8 * e].rearrange("p (h q) -> p h q", q=e)
                t2 = rpt[:, 512:512 + nh8 * e].rearrange("p (h q) -> p h q", q=e)
                dv = dst_bf_1024.rearrange("p (h q) -> p h q", q=128)
                o1 = dv[:, :, 0:e]
                o2 = dv[:, :, e:128]
                rd = list(src_deps) + [r_cos, r_sin]
                P.op(DVE, lambda eo: eo.tensor_tensor(t1, k1, cb, ALU.mult),
                     reads=rd, writes=[r_tmp2])
                P.op(DVE, lambda eo: eo.tensor_tensor(t2, k2, sbn, ALU.mult),
                     reads=rd, writes=[r_tmp2])
                P.op(DVE, lambda eo: eo.tensor_tensor(o1, t1, t2, ALU.subtract),
                     reads=[r_tmp2], writes=[dst_res])
                P.op(DVE, lambda eo: eo.tensor_tensor(t1, k2, cb, ALU.mult),
                     reads=rd, writes=[r_tmp2])
                P.op(DVE, lambda eo: eo.tensor_tensor(t2, k1, sbn, ALU.mult),
                     reads=rd, writes=[r_tmp2])
                P.op(DVE, lambda eo: eo.tensor_tensor(o2, t1, t2, ALU.add),
                     reads=[r_tmp2], writes=[dst_res])

            r_tmp2 = r_tmp

            for seq in range(B):
                for g in range(4):
                    for t in range(4):
                        kt = g * 4 + t
                        xs = t % 2
                        xap = xbuf[:, xs * D:(xs + 1) * D]
                        aap = abf[:, xs * D:(xs + 1) * D]
                        P.op(SP, lambda eo, seq=seq, kt=kt, xap=xap:
                             eo.dma_start(xap,
                                          x_in[seq, kt * 128:kt * 128 + 128, :]),
                             writes=[r_xb[xs]], dma=True, dslot=xs)
                        rs, nb, rstat = ln_stats(xap, [r_xb[xs]], (t % 2) * 8)
                        ln_apply(xap, [r_xb[xs]], aap, r_ab[xs], rs, nb, rstat)
                        transpose16(aap, [r_ab[xs]],
                                    afm[:, t * D:(t + 1) * D], r_afm[t], t)
                    for dc in range(16):
                        ws = dc % 3
                        wap = wring[:, ws * D:ws * D + 1024]
                        P.op(SP, lambda eo, dc=dc, wap=wap: eo.dma_start(
                            wap[:, 0:512],
                            wk_bf[0:DKV, dc * 128:(dc + 1) * 128],
                            transpose=True),
                            reads=[r_wkbf], writes=[r_wr[ws]], dma=True,
                            dslot=2 + ws)
                        P.op(SP, lambda eo, dc=dc, wap=wap: eo.dma_start(
                            wap[:, 512:1024],
                            wv_bf[0:DKV, dc * 128:(dc + 1) * 128],
                            transpose=True),
                            reads=[r_wvbf], writes=[r_wr[ws]], dma=True,
                            dslot=2 + ws, join=True)
                        for t in range(4):
                            ps_f, _, banks = pslot(t)
                            for n in range(2):
                                P.op(TE, lambda eo, t=t, dc=dc, n=n, wap=wap,
                                     ps_f=ps_f: eo.matmul(
                                         ps_f[:, n * 512:(n + 1) * 512],
                                         afm[:, t * D + dc * 128:
                                             t * D + (dc + 1) * 128],
                                         wap[:, n * 512:(n + 1) * 512],
                                         start=(dc == 0), stop=(dc == 15)),
                                     reads=[r_afm[t], r_wr[ws]], writes=banks)
                    for t in range(4):
                        kt = g * 4 + t
                        ps_f, ps_b, banks = pslot(t)
                        ks = 0
                        kap = krop[:, 0:DKV]
                        rope6(ps_f[:, 0:512], banks, NKV, kt, kap, r_kr[ks])
                        P.op(DVE, lambda eo, kt=kt, ps_f=ps_f: eo.tensor_copy(
                            vtm[:, kt * DKV:(kt + 1) * DKV],
                            ps_f[:, 512:1024]),
                            reads=banks, writes=[r_vtm])
                        for c in range(4):
                            P.op(TE, lambda eo, c=c, kap=kap, ps_b=ps_b:
                                 eo.transpose(ps_b[:, c * 128:(c + 1) * 128],
                                              kap[:, c * 128:(c + 1) * 128],
                                              ident[:]),
                                 reads=[r_kr[ks], r_ident], writes=banks)
                        P.op(DVE, lambda eo, kt=kt, ps_b=ps_b: eo.tensor_copy(
                            kfm[:].rearrange("p (h k) -> p h k", k=S)
                                  [:, :, kt * 128:kt * 128 + 128],
                            ps_b[:, 0:512].rearrange("p (h k) -> p h k", k=128)),
                            reads=banks, writes=[r_kfm])
                        pump_quant(1)

                if seq == 0 and "d_kfm" in dbg_out:
                    P.op(POOL, lambda eo: eo.dma_start(dbg_out["d_kfm"], kfm[:]),
                         reads=[r_kfm], dma=True)
                    P.op(POOL, lambda eo: eo.dma_start(dbg_out["d_vtm"], vtm[:]),
                         reads=[r_vtm], dma=True)

                for qtile in range(S // QT):
                    for g2 in range(QT // 512):
                        for t in range(4):
                            tsub = g2 * 4 + t
                            row0 = qtile * QT + tsub * 128
                            xs = t % 2
                            xap = xbuf[:, xs * D:(xs + 1) * D]
                            aap = abf[:, xs * D:(xs + 1) * D]
                            P.op(SP, lambda eo, seq=seq, row0=row0, xap=xap:
                                 eo.dma_start(xap,
                                              x_in[seq, row0:row0 + 128, :]),
                                 writes=[r_xb[xs]], dma=True, dslot=xs)
                            rs, nb, rstat = ln_stats(xap, [r_xb[xs]],
                                                     (t % 2) * 8)
                            ln_apply(xap, [r_xb[xs]], aap, r_ab[xs],
                                     rs, nb, rstat)
                            transpose16(aap, [r_ab[xs]],
                                        afm[:, t * D:(t + 1) * D],
                                        r_afm[t], t)
                        for half in range(2):
                            for dc in range(16):
                                ws = dc % 3
                                wap = wring[:, ws * D:ws * D + 1024]
                                P.op(SP, lambda eo, dc=dc, half=half, wap=wap:
                                     eo.dma_start(
                                         wap,
                                         wq_bf[half * 1024:(half + 1) * 1024,
                                               dc * 128:(dc + 1) * 128],
                                         transpose=True),
                                     reads=[r_wqbf], writes=[r_wr[ws]],
                                     dma=True, dslot=2 + ws)
                                for t in range(4):
                                    ps_f, _, banks = pslot(t)
                                    for n in range(2):
                                        P.op(TE, lambda eo, t=t, dc=dc, n=n,
                                             wap=wap, ps_f=ps_f: eo.matmul(
                                                 ps_f[:, n * 512:(n + 1) * 512],
                                                 afm[:, t * D + dc * 128:
                                                     t * D + (dc + 1) * 128],
                                                 wap[:, n * 512:(n + 1) * 512],
                                                 start=(dc == 0),
                                                 stop=(dc == 15)),
                                             reads=[r_afm[t], r_wr[ws]],
                                             writes=banks)
                            for t in range(4):
                                tsub = g2 * 4 + t
                                ktq = qtile * (QT // 128) + tsub
                                ps_f, _, banks = pslot(t)
                                rope6(ps_f[:, 0:1024], banks, 8, ktq,
                                      qbf[:, t * D + half * 1024:
                                          t * D + (half + 1) * 1024],
                                      r_qb[t])
                        for t in range(4):
                            tsub = g2 * 4 + t
                            _, ps_b, banks = pslot(t)
                            for c in range(16):
                                P.op(TE, lambda eo, c=c, t=t, ps_b=ps_b:
                                     eo.transpose(
                                         ps_b[:, c * 128:(c + 1) * 128],
                                         qbf[:, t * D + c * 128:
                                             t * D + (c + 1) * 128],
                                         ident[:]),
                                     reads=[r_qb[t], r_ident], writes=banks)
                            P.op(DVE, lambda eo, tsub=tsub, ps_b=ps_b:
                                 eo.tensor_copy(
                                     qfm[:].rearrange("p (h q) -> p h q", q=QT)
                                           [:, :, tsub * 128:tsub * 128 + 128],
                                     ps_b[:, 0:2048].rearrange(
                                         "p (h q) -> p h q", q=128)),
                                 reads=banks, writes=[r_qfm])
                        pump_quant(2)

                    if seq == 0 and qtile == 0 and "d_qfm" in dbg_out:
                        P.op(POOL, lambda eo: eo.dma_start(dbg_out["d_qfm"],
                                                           qfm[:]),
                             reads=[r_qfm], dma=True)

                    for kvh in range(NKV):
                        for hh in range(4):
                            h = kvh * 4 + hh
                            oS = [pslot(0), pslot(1)]
                            oO_f, _, oO_banks = pslot(2)
                            dn_f, _, dn_banks = pslot(3)
                            for kt in range(NKT):
                                s_f, _, s_banks = oS[kt % 2]
                                es = kt % 2
                                eap = expb[:, es * QT:(es + 1) * QT]
                                for n in range(QT // 512):
                                    P.op(TE, lambda eo, kvh=kvh, h=h, kt=kt,
                                         n=n, s_f=s_f: eo.matmul(
                                             s_f[:, n * 512:(n + 1) * 512],
                                             kfm[:, kvh * S + kt * 128:
                                                 kvh * S + (kt + 1) * 128],
                                             qfm[:, h * QT + n * 512:
                                                 h * QT + (n + 1) * 512]),
                                         reads=[r_kfm, r_qfm], writes=s_banks)
                                P.op(ACT, lambda eo, s_f=s_f, eap=eap:
                                     eo.activation(eap, s_f[:, 0:QT], AF.Exp,
                                                   scale=ISQ),
                                     reads=s_banks, writes=[r_ex[es]])
                                if kt == 0:
                                    P.op(POOL, lambda eo, eap=eap:
                                         eo.tensor_copy(dacc[:], eap),
                                         reads=[r_ex[es]], writes=[r_da])
                                else:
                                    P.op(POOL, lambda eo, eap=eap:
                                         eo.tensor_tensor(dacc[:], dacc[:],
                                                          eap, ALU.add),
                                         reads=[r_ex[es], r_da],
                                         writes=[r_da])
                                for n in range(QT // 512):
                                    P.op(TE, lambda eo, kvh=kvh, kt=kt, n=n,
                                         eap=eap, oO_f=oO_f: eo.matmul(
                                             oO_f[:, n * 512:(n + 1) * 512],
                                             vtm[:, kt * DKV + kvh * 128:
                                                 kt * DKV + (kvh + 1) * 128],
                                             eap[:, n * 512:(n + 1) * 512],
                                             start=(kt == 0),
                                             stop=(kt == NKT - 1)),
                                         reads=[r_vtm, r_ex[es]],
                                         writes=oO_banks)
                            for n in range(QT // 512):
                                P.op(TE, lambda eo, n=n, dn_f=dn_f: eo.matmul(
                                    dn_f[0:1, n * 512:(n + 1) * 512],
                                    ones_col[:], dacc[:, n * 512:(n + 1) * 512]),
                                    reads=[r_da, r_ones], writes=dn_banks)
                            P.op(DVE, lambda eo, dn_f=dn_f: eo.tensor_copy(
                                dsb[0:1, 0:QT], dn_f[0:1, 0:QT]),
                                reads=dn_banks, writes=[r_ds])
                            P.op(DVE, lambda eo: eo.reciprocal(
                                rden[0:1, 0:QT], dsb[0:1, 0:QT]),
                                reads=[r_ds], writes=[r_rd])
                            for n in range(QT // 512):
                                P.op(TE, lambda eo, n=n, dn_f=dn_f: eo.matmul(
                                    dn_f[:, n * 512:(n + 1) * 512],
                                    ones_row[:],
                                    rden[0:1, n * 512:(n + 1) * 512]),
                                    reads=[r_rd, r_ones], writes=dn_banks)
                            P.op(DVE, lambda eo, dn_f=dn_f: eo.tensor_copy(
                                rden[:], dn_f[:, 0:QT]),
                                reads=dn_banks, writes=[r_rd])
                            P.op(DVE, lambda eo, h=h, oO_f=oO_f:
                                 eo.tensor_tensor(
                                     ofm[:, h * QT:(h + 1) * QT],
                                     oO_f[:, 0:QT], rden[:], ALU.mult),
                                 reads=oO_banks + [r_rd], writes=[r_ofm])
                            pump_quant(2)

                    if seq == 0 and qtile == 0 and "d_ofm" in dbg_out:
                        P.op(POOL, lambda eo: eo.dma_start(dbg_out["d_ofm"],
                                                           ofm[:]),
                             reads=[r_ofm], dma=True)

                    for g2 in range(QT // 512):
                        for half in range(2):
                            for c in range(16):
                                ws = c % 3
                                wap = wring[:, ws * D:ws * D + 1024]
                                P.op(SP, lambda eo, c=c, half=half, wap=wap:
                                     eo.dma_start(
                                         wap,
                                         wo_bf[half * 1024:(half + 1) * 1024,
                                               c * 128:(c + 1) * 128],
                                         transpose=True),
                                     reads=[r_wobf], writes=[r_wr[ws]],
                                     dma=True, dslot=2 + ws)
                                for t in range(4):
                                    tsub = g2 * 4 + t
                                    ps_f, _, banks = pslot(t)
                                    for n in range(2):
                                        P.op(TE, lambda eo, c=c, tsub=tsub,
                                             n=n, wap=wap, ps_f=ps_f:
                                             eo.matmul(
                                                 ps_f[:, n * 512:(n + 1) * 512],
                                                 ofm[:, c * QT + tsub * 128:
                                                     c * QT + tsub * 128 + 128],
                                                 wap[:, n * 512:(n + 1) * 512],
                                                 start=(c == 0),
                                                 stop=(c == 15)),
                                             reads=[r_ofm, r_wr[ws]],
                                             writes=banks)
                            for t in range(4):
                                ps_f, _, banks = pslot(t)
                                col = 16 + t * 16
                                su = stt[:, col + 4 * half:col + 4 * half + 1]
                                sq = stt[:, col + 4 * half + 1:
                                         col + 4 * half + 2]
                                P.op(DVE, lambda eo, ps_f=ps_f, su=su:
                                     eo.tensor_reduce(su, ps_f[:, 0:1024],
                                                      AX.X, ALU.add),
                                     reads=banks, writes=[r_st[8 + t]])
                                if half == 0:
                                    P.op(DVE, lambda eo, t=t, ps_f=ps_f:
                                         eo.tensor_copy(
                                             fstage[:, t * 1024:
                                                    (t + 1) * 1024],
                                             ps_f[:, 0:1024]),
                                         reads=banks, writes=[r_fs[t]])
                                extra = [r_fs[t]] if half == 0 else []
                                P.op(ACT, lambda eo, ps_f=ps_f, sq=sq:
                                     eo.activation(sqscr[:, 0:1024],
                                                   ps_f[:, 0:1024], AF.Square,
                                                   accum_out=sq),
                                     reads=banks + extra,
                                     writes=[r_st[8 + t], r_sqscr])
                        for t in range(4):
                            tsub = g2 * 4 + t
                            row0 = qtile * QT + tsub * 128
                            grow = seq * S + row0
                            ps_f, _, banks = pslot(t)
                            col = 16 + t * 16
                            su0 = stt[:, col + 0:col + 1]
                            sq0 = stt[:, col + 1:col + 2]
                            su1 = stt[:, col + 4:col + 5]
                            sq1 = stt[:, col + 5:col + 6]
                            m = stt[:, col + 8:col + 9]
                            ms = stt[:, col + 9:col + 10]
                            v = stt[:, col + 10:col + 11]
                            v2 = stt[:, col + 11:col + 12]
                            rs = stt[:, col + 12:col + 13]
                            nb = stt[:, col + 13:col + 14]
                            rstat = r_st[8 + t]
                            P.op(DVE, lambda eo, su0=su0, su1=su1, m=m:
                                 eo.tensor_tensor(m, su0, su1, ALU.add),
                                 reads=[rstat], writes=[rstat])
                            P.op(DVE, lambda eo, m=m: eo.tensor_scalar(
                                m, m, 1.0 / D, None, ALU.mult),
                                reads=[rstat], writes=[rstat])
                            P.op(DVE, lambda eo, sq0=sq0, sq1=sq1, v=v:
                                 eo.tensor_tensor(v, sq0, sq1, ALU.add),
                                 reads=[rstat], writes=[rstat])
                            P.op(DVE, lambda eo, v=v: eo.tensor_scalar(
                                v, v, 1.0 / D, None, ALU.mult),
                                reads=[rstat], writes=[rstat])
                            P.op(DVE, lambda eo, m=m, ms=ms: eo.tensor_scalar(
                                ms, m, m, None, ALU.mult),
                                reads=[rstat], writes=[rstat])
                            P.op(DVE, lambda eo, v=v, ms=ms, v2=v2:
                                 eo.tensor_tensor(v2, v, ms, ALU.subtract),
                                 reads=[rstat], writes=[rstat])
                            P.op(ACT, lambda eo, v2=v2, ms=ms: eo.activation(
                                ms, v2, AF.Ln, bias=cns[:, 1:2]),
                                reads=[rstat, r_cns], writes=[rstat])
                            P.op(ACT, lambda eo, ms=ms, rs=rs: eo.activation(
                                rs, ms, AF.Exp, scale=-0.5),
                                reads=[rstat], writes=[rstat])
                            P.op(DVE, lambda eo, m=m, rs=rs, nb=nb:
                                 eo.tensor_scalar(nb, m, rs, -1.0,
                                                  ALU.mult, ALU.mult),
                                 reads=[rstat], writes=[rstat])
                            xs = t % 2
                            xap = xbuf[:, xs * D:(xs + 1) * D]
                            P.op(SP, lambda eo, seq=seq, row0=row0, xap=xap:
                                 eo.dma_start(xap,
                                              x_in[seq, row0:row0 + 128, :]),
                                 writes=[r_xb[xs]], dma=True, dslot=xs)
                            P.op(ACT, lambda eo, t=t, rs=rs, nb=nb:
                                 eo.activation(x1st[:, 0:1024],
                                               fstage[:, t * 1024:
                                                      (t + 1) * 1024],
                                               AF.Identity, bias=nb, scale=rs),
                                 reads=[r_fs[t], rstat], writes=[r_x1st])
                            P.op(ACT, lambda eo, ps_f=ps_f, rs=rs, nb=nb:
                                 eo.activation(x1st[:, 1024:2048],
                                               ps_f[:, 0:1024],
                                               AF.Identity, bias=nb, scale=rs),
                                 reads=banks + [rstat], writes=[r_x1st])
                            P.op(DVE, lambda eo, xap=xap: eo.tensor_tensor(
                                x1o[:], x1st[:], xap, ALU.add),
                                reads=[r_x1st, r_xb[xs]], writes=[r_x1o])
                            P.op(SP, lambda eo, grow=grow: eo.dma_start(
                                x1scr[grow:grow + 128, :], x1o[:]),
                                reads=[r_x1o], writes=[r_x1[grow // 128]],
                                dma=True, dslot=6)
                        pump_quant(2)

            while quant_jobs:
                emit_quant(quant_jobs.pop(0))

            if "d_x1" in dbg_out:
                P.op(SP, lambda eo: eo.dma_start(dbg_out["d_x1"], x1scr),
                     reads=r_x1, dma=True)
            if "d_wgu" in dbg_out:
                P.op(SP, lambda eo: eo.dma_start(dbg_out["d_wgu"], wgu_h),
                     reads=[r_wguh], dma=True)
                P.op(SP, lambda eo: eo.dma_start(dbg_out["d_wp"], wp_h),
                     reads=[r_wph], dma=True)

        P.barrier("attn->ffn")

        with contextlib.ExitStack() as sf:
            def sbf(name, shape, dt):
                return sf.enter_context(nc.sbuf_tensor(name, shape, dt))

            gur = sbf("gur", [128, 3 * 16 * 512], BF16)
            wpr = sbf("wpr", [128, 3 * D], BF16)
            hfm = sbf("hfm", [128, 64 * TFF], BF16)
            ffm = sbf("ffm", [128, 16 * TFF], BF16)
            ftmp = sbf("ftmp", [128, 512], F32)
            ffsb = sbf("ffsb", [128, D], F32)
            rsums = sbf("rsums", [128, 8], F32)
            dsb_ffn = sbf("dsb_ffn", [1, 512], F32)

            for Ti in range(TOK // TFF):
                trow = Ti * TFF
                for ts in range(4):
                    grow = trow + ts * 128
                    xs = ts % 2
                    xap = xbuf[:, xs * D:(xs + 1) * D]
                    aap = abf[:, xs * D:(xs + 1) * D]
                    P.op(SP, lambda eo, grow=grow, xap=xap: eo.dma_start(
                        xap, x1scr[grow:grow + 128, :]),
                        reads=[r_x1[grow // 128]], writes=[r_xb[xs]],
                        dma=True, dslot=xs)
                    rs, nb, rstat = ln_stats(xap, [r_xb[xs]], (ts % 2) * 8)
                    ln_apply(xap, [r_xb[xs]], aap, r_ab[xs], rs, nb, rstat)
                    ps_f, ps_b, banks = pslot(2 + ts % 2)
                    for c in range(16):
                        P.op(TE, lambda eo, c=c, aap=aap, ps_b=ps_b:
                             eo.transpose(ps_b[:, c * 128:(c + 1) * 128],
                                          aap[:, c * 128:(c + 1) * 128],
                                          ident[:]),
                             reads=[r_ab[xs], r_ident], writes=banks)
                    P.op(DVE, lambda eo, ts=ts, ps_b=ps_b: eo.tensor_copy(
                        ffm[:].rearrange("p (dc t) -> p dc t", t=TFF)
                              [:, :, ts * 128:ts * 128 + 128],
                        ps_b[:, 0:2048].rearrange("p (dc t) -> p dc t", t=128)),
                        reads=banks, writes=[r_ffm])

                sq_f, sq_banks = pbank(4)
                nsl = 0
                for s in range(16):
                    for gu_i in range(2):
                        sl = nsl % 3
                        nsl += 1
                        hrow0 = gu_i * HFF + s * 512
                        base = sl * 16 * 512
                        for dc in range(16):
                            P.op(SP, lambda eo, hrow0=hrow0, dc=dc, base=base:
                                 eo.dma_start(
                                     gur[:, base + dc * 512:
                                         base + (dc + 1) * 512],
                                     wgu_h[hrow0:hrow0 + 512,
                                           dc * 128:(dc + 1) * 128],
                                     transpose=True),
                                 reads=[r_wguh], writes=[r_gur[sl]], dma=True,
                                 dslot=2 + sl, join=(dc > 0))
                        if gu_i == 0:
                            gslot = sl
                        else:
                            uslot = sl
                    gbase = gslot * 16 * 512
                    ubase = uslot * 16 * 512
                    for hsub in range(4):
                        hc = s * 4 + hsub
                        pg_f, pg_banks = pbank((hsub % 2) * 2)
                        pu_f, pu_banks = pbank((hsub % 2) * 2 + 1)
                        for dc in range(16):
                            P.op(TE, lambda eo, dc=dc, hsub=hsub, gbase=gbase,
                                 pg_f=pg_f: eo.matmul(
                                     pg_f,
                                     gur[:, gbase + dc * 512 + hsub * 128:
                                         gbase + dc * 512 + hsub * 128 + 128],
                                     ffm[:, dc * TFF:(dc + 1) * TFF],
                                     start=(dc == 0), stop=(dc == 15)),
                                 reads=[r_gur[gslot], r_ffm], writes=pg_banks)
                            P.op(TE, lambda eo, dc=dc, hsub=hsub, ubase=ubase,
                                 pu_f=pu_f: eo.matmul(
                                     pu_f,
                                     gur[:, ubase + dc * 512 + hsub * 128:
                                         ubase + dc * 512 + hsub * 128 + 128],
                                     ffm[:, dc * TFF:(dc + 1) * TFF],
                                     start=(dc == 0), stop=(dc == 15)),
                                 reads=[r_gur[uslot], r_ffm], writes=pu_banks)
                        hap = hfm[:, hc * TFF:(hc + 1) * TFF]
                        P.op(ACT, lambda eo, pg_f=pg_f: eo.activation(
                            ftmp[:], pg_f, AF.Silu),
                            reads=pg_banks, writes=[r_ftmp])
                        P.op(DVE, lambda eo, hap=hap, pu_f=pu_f:
                             eo.tensor_tensor(hap, ftmp[:], pu_f, ALU.mult),
                             reads=[r_ftmp] + pu_banks, writes=[r_hfm])
                        sqs = hc % 2
                        P.op(ACT, lambda eo, hap=hap, sqs=sqs: eo.activation(
                            sqscr[:, sqs * 512:(sqs + 1) * 512], hap,
                            AF.Square),
                            reads=[r_hfm], writes=[r_sqb[sqs], r_sqscr])
                        P.op(TE, lambda eo, sqs=sqs, hc=hc, sq_f=sq_f:
                             eo.matmul(sq_f[0:1, 0:512], ones_bcol[:],
                                       sqscr[:, sqs * 512:(sqs + 1) * 512],
                                       start=(hc == 0), stop=(hc == 63)),
                             reads=[r_sqb[sqs], r_ones], writes=sq_banks)

                P.op(DVE, lambda eo, sq_f=sq_f: eo.tensor_copy(
                    dsb_ffn[0:1, 0:512], sq_f[0:1, 0:512]),
                    reads=sq_banks, writes=[r_rst])
                tr_f, tr_b, tr_banks = pslot(3)
                for c in range(4):
                    P.op(TE, lambda eo, c=c, tr_f=tr_f: eo.transpose(
                        tr_f[:, c:c + 1],
                        dsb_ffn[0:1, c * 128:(c + 1) * 128],
                        identf[0:1, 0:1]),
                        reads=[r_rst, r_identf], writes=tr_banks)
                P.op(DVE, lambda eo, tr_f=tr_f: eo.tensor_copy(
                    rsums[:, 0:4], tr_f[:, 0:4]),
                    reads=tr_banks, writes=[r_rst])
                P.op(ACT, lambda eo: eo.activation(
                    rsums[:, 4:8], rsums[:, 0:4], AF.Ln,
                    scale=1.0 / HFF, bias=cns[:, 2:3]),
                    reads=[r_rst, r_cns], writes=[r_rst])
                P.op(ACT, lambda eo: eo.activation(
                    rsums[:, 0:4], rsums[:, 4:8], AF.Exp, scale=-0.5),
                    reads=[r_rst], writes=[r_rst])

                if Ti == 0 and "d_hfm" in dbg_out:
                    P.op(POOL, lambda eo: eo.dma_start(dbg_out["d_hfm"],
                                                       hfm[:]),
                         reads=[r_hfm], dma=True)

                for pr in range(2):
                    pboth = [(pA[:, 0:2048], [bankres[k] for k in range(4)]),
                             (pB[:, 0:2048], [bankres[4 + k] for k in range(4)])]
                    for hc in range(64):
                        ws = hc % 3
                        wap = wpr[:, ws * D:(ws + 1) * D]
                        P.op(SP, lambda eo, hc=hc, wap=wap: eo.dma_start(
                            wap, wp_h[0:D, hc * 128:(hc + 1) * 128],
                            transpose=True),
                            reads=[r_wph], writes=[r_wpr[ws]], dma=True,
                            dslot=8 + ws)
                        for pi in range(2):
                            ts = pr * 2 + pi
                            pp_f, pp_banks = pboth[pi]
                            for n in range(4):
                                P.op(TE, lambda eo, hc=hc, ts=ts, n=n,
                                     wap=wap, pp_f=pp_f: eo.matmul(
                                         pp_f[:, n * 512:(n + 1) * 512],
                                         hfm[:, hc * TFF + ts * 128:
                                             hc * TFF + ts * 128 + 128],
                                         wap[:, n * 512:(n + 1) * 512],
                                         start=(hc == 0), stop=(hc == 63)),
                                     reads=[r_hfm, r_wpr[ws]],
                                     writes=pp_banks)
                    for pi in range(2):
                        ts = pr * 2 + pi
                        grow = trow + ts * 128
                        b0, s0 = divmod(grow, S)
                        pp_f, pp_banks = pboth[pi]
                        rsap = rsums[:, ts:ts + 1]
                        P.op(DVE, lambda eo, pp_f=pp_f, rsap=rsap:
                             eo.tensor_scalar(ffsb[:], pp_f, rsap, None,
                                              ALU.mult),
                             reads=pp_banks + [r_rst], writes=[r_ffsb])
                        rs, nb, rstat = ln_stats(ffsb[:], [r_ffsb], 64)
                        xs = ts % 2
                        xap = xbuf[:, xs * D:(xs + 1) * D]
                        P.op(SP, lambda eo, grow=grow, xap=xap: eo.dma_start(
                            xap, x1scr[grow:grow + 128, :]),
                            reads=[r_x1[grow // 128]], writes=[r_xb[xs]],
                            dma=True, dslot=xs)
                        P.op(ACT, lambda eo, rs=rs, nb=nb: eo.activation(
                            x1st[:], ffsb[:], AF.Identity, bias=nb, scale=rs),
                            reads=[r_ffsb, rstat], writes=[r_x1st])
                        P.op(DVE, lambda eo, xap=xap: eo.tensor_tensor(
                            ffsb[:], x1st[:], xap, ALU.add),
                            reads=[r_x1st, r_xb[xs]], writes=[r_ffsb])
                        P.op(SP, lambda eo, b0=b0, s0=s0: eo.dma_start(
                            y_out[b0, s0:s0 + 128, :], ffsb[:]),
                            reads=[r_ffsb], dma=True, dslot=7)

        P.barrier("end")

        with nc.Block() as block:
            P.lower(nc, block, csem, dsem)

    return nc



def kernel(**inputs):
    x = np.ascontiguousarray(inputs["L0_attn_hidden_states"])
    in_common = {
        "wq": np.ascontiguousarray(inputs["L0_attn_Wq"]),
        "wk": np.ascontiguousarray(inputs["L0_attn_Wk"]),
        "wv": np.ascontiguousarray(inputs["L0_attn_Wv"]),
        "wo": np.ascontiguousarray(inputs["L0_attn_Wo"]),
        "wgu": np.ascontiguousarray(inputs["L0_ffn_w_gate_up"]),
        "wp": np.ascontiguousarray(inputs["L0_ffn_w_proj"]),
    }
    nc = build_program()
    in_maps = []
    for c in range(8):
        m = dict(in_common)
        m["x"] = np.ascontiguousarray(x[c * B:(c + 1) * B])
        in_maps.append(m)
    res = run_bass_kernel_spmd(nc, in_maps, core_ids=list(range(8)))
    y = np.concatenate([r["y"] for r in res.results], axis=0)
    return y


# version 04 (final)
# speedup vs baseline: 1.0268x; 1.0268x over previous
import contextlib
import numpy as np

import concourse.bass as bass
import concourse.mybir as mybir
from concourse.bass_utils import run_bass_kernel_spmd

F32 = mybir.dt.float32
BF16 = mybir.dt.bfloat16
AF = mybir.ActivationFunctionType
ALU = mybir.AluOpType
AX = mybir.AxisListType

B = 2
S = 2048
D = 2048
HD = 128
NH = 16
NKV = 4
DKV = NKV * HD
HFF = 8192
TOK = B * S
NKT = S // 128
QT = 512
TFF = 512

LN_EPS = 1e-5
RMS_EPS = 1.1920929e-07
ISQ = 0.08838834764831845
QTHR = 0.5009765625
MAGIC = 12582912.0
TWO_PI_HI = 6.28125
TWO_PI_LO = 0.0019353071795864769
INV_2PI = 0.15915494309189535
LN1E4 = 9.210340371976184
HALF_PI = 1.5707963267948966

ENGINES = ("TE", "ACT", "DVE", "POOL", "SP")
_BLOCK = dict(TE="tensor", ACT="scalar", DVE="vector", POOL="gpsimd", SP="sync")
TE, ACT, DVE, POOL, SP = ENGINES


class Res(object):
    __slots__ = ("w", "r", "burst")

    def __init__(self):
        self.w = None
        self.r = []
        self.burst = set()


NDSLOT = 12


class Plan(object):
    def __init__(self):
        self.ops = []
        self.eng = {e: [] for e in ENGINES}
        self.last_dma = {}
        self.fence = {e: -1 for e in ENGINES}

    def op(self, eng, emit, reads=(), writes=(), dma=False, deps=(), tag="",
           dslot=0, join=False):
        i = len(self.ops)
        dep = set(d for d in deps if d is not None)
        for res in reads:
            if res.w is not None:
                dep.add(res.w)
        for res in writes:
            if join:
                dep.update(res.burst)
            else:
                dep.update(res.r)
                if res.w is not None:
                    dep.add(res.w)
        rec = {"i": i, "eng": eng, "emit": emit, "deps": dep, "dma": dma,
               "inc": False, "pos": len(self.eng[eng]), "tag": tag,
               "slot": dslot if dma else None, "fence": self.fence[eng]}
        self.ops.append(rec)
        self.eng[eng].append(rec)
        if dma:
            self.last_dma[(eng, dslot)] = i
        for res in reads:
            res.r.append(i)
        for res in writes:
            if not join:
                res.burst = set(res.r)
                if res.w is not None:
                    res.burst.add(res.w)
                res.r = []
            res.w = i
        return i

    def barrier(self, tag="barrier"):
        ids = []
        for e in ENGINES:
            if self.eng[e]:
                ids.append(self.eng[e][-1]["i"])
        ids.extend(self.last_dma.values())
        ids = sorted(set(ids))
        bstart = len(self.ops)
        for e in ENGINES:
            self.op(e, lambda eo: eo.nop(), deps=ids, tag=tag)
        for e in ENGINES:
            self.fence[e] = bstart

    def lower(self, nc, block, csem, dsem):
        for op in self.ops:
            best = {}
            for d in op["deps"]:
                if d < op["fence"]:
                    continue
                p = self.ops[d]
                key = (p["eng"], p["dma"], p["slot"])
                q = best.get(key)
                if q is None or p["pos"] > q["pos"]:
                    best[key] = p
            wl = []
            for (peng, pdma, pslot_), p in best.items():
                if peng == op["eng"] and not pdma and peng == TE:
                    continue
                p["inc"] = True
                wl.append(p)
            op["waits"] = wl
        for e in ENGINES:
            c = 0
            dc = [0] * NDSLOT
            for op in self.eng[e]:
                if op["dma"]:
                    s = op["slot"]
                    dc[s] += 16
                    op["val"] = dc[s]
                    op["sem"] = dsem[e][s]
                elif op["inc"]:
                    c += 1
                    op["val"] = c
                    op["sem"] = csem[e]

        def make(eng):
            def body(eo):
                hwm = {}
                for op in self.eng[eng]:
                    for p in op["waits"]:
                        s = p["sem"]
                        v = p["val"]
                        k = id(s)
                        if hwm.get(k, 0) < v:
                            eo.wait_ge(s, v)
                            hwm[k] = v
                    ins = op["emit"](eo)
                    if op["dma"]:
                        ins.then_inc(op["sem"], 16)
                    elif op["inc"]:
                        ins.then_inc(op["sem"], 1)
            return body
        for e in ENGINES:
            getattr(block, _BLOCK[e])(make(e))


def build_program(dbg=()):
    nc = bass.Bass("TRN2", target_bir_lowering=False)

    x_in = nc.dram_tensor("x", [B, S, D], F32, kind="ExternalInput").ap()
    wq = nc.dram_tensor("wq", [D, D], F32, kind="ExternalInput").ap()
    wk = nc.dram_tensor("wk", [DKV, D], F32, kind="ExternalInput").ap()
    wv = nc.dram_tensor("wv", [DKV, D], F32, kind="ExternalInput").ap()
    wo = nc.dram_tensor("wo", [D, D], F32, kind="ExternalInput").ap()
    wgu = nc.dram_tensor("wgu", [2 * HFF, D], F32, kind="ExternalInput").ap()
    wp = nc.dram_tensor("wp", [D, HFF], F32, kind="ExternalInput").ap()
    y_out = nc.dram_tensor("y", [B, S, D], F32, kind="ExternalOutput").ap()

    wq_bf = nc.dram_tensor("wq_bf", [D, D], BF16, kind="Internal").ap()
    wk_bf = nc.dram_tensor("wk_bf", [DKV, D], BF16, kind="Internal").ap()
    wv_bf = nc.dram_tensor("wv_bf", [DKV, D], BF16, kind="Internal").ap()
    wo_bf = nc.dram_tensor("wo_bf", [D, D], BF16, kind="Internal").ap()
    wgu_h = nc.dram_tensor("wgu_h", [2 * HFF, D], BF16, kind="Internal").ap()
    wp_h = nc.dram_tensor("wp_h", [D, HFF], BF16, kind="Internal").ap()
    x1scr = nc.dram_tensor("x1scr", [TOK, D], F32, kind="Internal").ap()

    dbg_out = {}
    for name, shape, dt in (
        ("d_cos", [128, NKT * 64], F32), ("d_sin", [128, NKT * 64], F32),
        ("d_kfm", [128, NKV * S], F32), ("d_vtm", [128, NKT * DKV], F32),
        ("d_qfm", [128, NH * QT], F32), ("d_ofm", [128, NH * QT], F32),
        ("d_x1", [TOK, D], F32),
        ("d_wgu", [2 * HFF, D], BF16), ("d_wp", [D, HFF], BF16),
        ("d_hfm", [128, 64 * TFF], F32),
    ):
        if name in dbg:
            dbg_out[name] = nc.dram_tensor(name, shape, dt,
                                           kind="ExternalOutput").ap()

    P = Plan()

    r_ident = Res()
    r_identf = Res()
    r_ones = Res()
    r_cns = Res()
    r_sqscr = Res()
    r_wqbf, r_wkbf, r_wvbf, r_wobf = Res(), Res(), Res(), Res()
    r_wguh = Res()
    r_wph = Res()
    r_x1 = [Res() for _ in range(TOK // 128)]
    r_cos, r_sin, r_tmp = Res(), Res(), Res()
    r_kfm, r_vtm, r_qfm, r_ofm = Res(), Res(), Res(), Res()
    r_xb = [Res(), Res()]
    r_ab = [Res(), Res()]
    r_afm = [Res() for _ in range(4)]
    r_qb = [Res() for _ in range(4)]
    r_kr = [Res(), Res()]
    r_ex = [Res(), Res()]
    r_da, r_ds, r_rd = Res(), Res(), Res()
    r_wr = [Res(), Res(), Res()]
    r_fs = [Res() for _ in range(4)]
    r_st = [Res() for _ in range(16)]
    r_x1st = Res()
    r_x1o = Res()
    r_qt = Res()
    r_qt2 = [Res(), Res()]
    r_gur = [Res(), Res(), Res()]
    r_wpr = [Res(), Res(), Res()]
    r_hfm = Res()
    r_ffm = Res()
    r_ftmp = Res()
    r_ffsb = Res()
    r_sqb = [Res(), Res()]
    r_rst = Res()
    bankres = [Res() for _ in range(8)]

    with contextlib.ExitStack() as st:
        def sb(name, shape, dt):
            return st.enter_context(nc.sbuf_tensor(name, shape, dt))

        ident = sb("ident", [128, 128], BF16)
        identf = sb("identf", [128, 128], F32)
        ones_col = sb("ones_col", [128, 1], F32)
        ones_row = sb("ones_row", [1, 128], F32)
        ones_bcol = sb("ones_bcol", [128, 1], BF16)
        xbuf = sb("xbuf", [128, 2 * D], F32)
        abf = sb("abf", [128, 2 * D], BF16)
        sqscr = sb("sqscr", [128, D], BF16)
        stt = sb("stt", [128, 128], F32)
        x1st = sb("x1st", [128, D], F32)
        cns = sb("cns", [128, 3], F32)
        pA = st.enter_context(nc.psum_tensor("pA", [128, 2048], F32))
        pB = st.enter_context(nc.psum_tensor("pB", [128, 2048], F32))
        csem = {e: st.enter_context(nc.semaphore("c" + e)) for e in ENGINES}
        dsem = {e: [st.enter_context(nc.semaphore("d%s%d" % (e, s)))
                    for s in range(NDSLOT)]
                for e in (SP, POOL)}
        dsem[TE] = dsem[ACT] = dsem[DVE] = dsem[SP]

        def pslot(k):
            t = pA if k < 2 else pB
            o = (k % 2) * 1024
            return (t[:, o:o + 1024],
                    t[:].bitcast(BF16)[:, 2 * o:2 * o + 2048],
                    [bankres[2 * k], bankres[2 * k + 1]])

        def pbank(k):
            t = pA if k < 4 else pB
            o = (k % 4) * 512
            return t[:, o:o + 512], [bankres[k]]

        P.op(POOL, lambda eo: eo.memset(ident[:], 0.0), writes=[r_ident])
        P.op(POOL, lambda eo: eo.affine_select(
            out=ident[:], in_=ident[:], compare_op=ALU.not_equal, fill=1.0,
            base=0, pattern=[[-1, 128]], channel_multiplier=1),
            reads=[r_ident], writes=[r_ident])
        P.op(POOL, lambda eo: eo.memset(identf[:], 0.0), writes=[r_identf])
        P.op(POOL, lambda eo: eo.affine_select(
            out=identf[:], in_=identf[:], compare_op=ALU.not_equal, fill=1.0,
            base=0, pattern=[[-1, 128]], channel_multiplier=1),
            reads=[r_identf], writes=[r_identf])
        P.op(POOL, lambda eo: eo.memset(ones_col[:], 1.0), writes=[r_ones])
        P.op(POOL, lambda eo: eo.memset(ones_row[:], 1.0), writes=[r_ones])
        P.op(POOL, lambda eo: eo.memset(ones_bcol[:], 1.0), writes=[r_ones])
        P.op(POOL, lambda eo: eo.memset(cns[:, 0:1], HALF_PI), writes=[r_cns])
        P.op(POOL, lambda eo: eo.memset(cns[:, 1:2], LN_EPS), writes=[r_cns])
        P.op(POOL, lambda eo: eo.memset(cns[:, 2:3], RMS_EPS), writes=[r_cns])

        P.op(POOL, lambda eo: eo.dma_start(wq_bf, wq), writes=[r_wqbf],
             dma=True, dslot=0)
        P.op(POOL, lambda eo: eo.dma_start(wk_bf, wk), writes=[r_wkbf],
             dma=True, dslot=1)
        P.op(POOL, lambda eo: eo.dma_start(wv_bf, wv), writes=[r_wvbf],
             dma=True, dslot=2)
        P.op(POOL, lambda eo: eo.dma_start(wo_bf, wo), writes=[r_wobf],
             dma=True, dslot=3)

        def ln_stats(src_ap, src_deps, col, nelem=D):
            su = stt[:, col + 0:col + 1]
            sq = stt[:, col + 1:col + 2]
            m = stt[:, col + 2:col + 3]
            ms = stt[:, col + 3:col + 4]
            v = stt[:, col + 4:col + 5]
            v2 = stt[:, col + 5:col + 6]
            rs = stt[:, col + 6:col + 7]
            nb = stt[:, col + 7:col + 8]
            rstat = r_st[col // 8]
            P.op(DVE, lambda eo: eo.tensor_reduce(su, src_ap, AX.X, ALU.add),
                 reads=src_deps, writes=[rstat])
            P.op(ACT, lambda eo: eo.activation(
                sqscr[:, 0:nelem], src_ap, AF.Square, accum_out=sq),
                reads=src_deps, writes=[rstat, r_sqscr])
            P.op(DVE, lambda eo: eo.tensor_scalar(m, su, 1.0 / nelem, None,
                                                  ALU.mult),
                 reads=[rstat], writes=[rstat])
            P.op(DVE, lambda eo: eo.tensor_scalar(ms, m, m, None, ALU.mult),
                 reads=[rstat], writes=[rstat])
            P.op(DVE, lambda eo: eo.tensor_scalar(v, sq, 1.0 / nelem, None,
                                                  ALU.mult),
                 reads=[rstat], writes=[rstat])
            P.op(DVE, lambda eo: eo.tensor_tensor(v2, v, ms, ALU.subtract),
                 reads=[rstat], writes=[rstat])
            P.op(ACT, lambda eo: eo.activation(ms, v2, AF.Ln, bias=cns[:, 1:2]),
                 reads=[rstat, r_cns], writes=[rstat])
            P.op(ACT, lambda eo: eo.activation(rs, ms, AF.Exp, scale=-0.5),
                 reads=[rstat], writes=[rstat])
            P.op(DVE, lambda eo: eo.tensor_scalar(nb, m, rs, -1.0,
                                                  ALU.mult, ALU.mult),
                 reads=[rstat], writes=[rstat])
            return rs, nb, rstat

        def ln_apply(src_ap, src_deps, out_ap, out_res, rs, nb, rstat):
            P.op(ACT, lambda eo: eo.activation(out_ap, src_ap, AF.Identity,
                                               bias=nb, scale=rs),
                 reads=list(src_deps) + [rstat], writes=[out_res])

        def transpose16(src_bf, src_deps, dst_ap, dst_res, slot, nblk=16):
            ps_f, ps_b, banks = pslot(slot)
            for c in range(nblk):
                P.op(TE, lambda eo, c=c: eo.transpose(
                    ps_b[:, c * 128:(c + 1) * 128],
                    src_bf[:, c * 128:(c + 1) * 128], ident[:]),
                    reads=list(src_deps) + [r_ident], writes=banks)
            P.op(DVE, lambda eo: eo.tensor_copy(dst_ap, ps_b[:, 0:nblk * 128]),
                 reads=banks, writes=[dst_res])

        with contextlib.ExitStack() as sa:
            def sba(name, shape, dt):
                return sa.enter_context(nc.sbuf_tensor(name, shape, dt))

            cosb = sba("cosb", [128, NKT * 64], F32)
            sinb = sba("sinb", [128, NKT * 64], F32)
            kfm = sba("kfm", [128, NKV * S], BF16)
            vtm = sba("vtm", [128, NKT * DKV], BF16)
            qfm = sba("qfm", [128, NH * QT], BF16)
            ofm = sba("ofm", [128, NH * QT], BF16)
            afm = sba("afm", [128, 4 * D], BF16)
            qbf = sba("qbf", [128, 4 * D], BF16)
            krop = sba("krop", [128, 1 * DKV], BF16)
            rpt = sba("rpt", [128, 2 * 512], F32)
            expb = sba("expb", [128, 2 * QT], BF16)
            dacc = sba("dacc", [128, QT], F32)
            dsb = sba("dsb", [1, QT], F32)
            rden = sba("rden", [128, QT], F32)
            wring = sba("wring", [128, 3 * D], BF16)
            fstage = sba("fstage", [128, 4 * 1024], F32)
            x1o = sba("x1o", [128, D], F32)
            qwb = sba("qwb", [128, D], BF16)
            qaw = sba("qaw", [128, D], BF16)
            qm = sba("qm", [128, D], BF16)
            qs = sba("qs", [128, D], BF16)
            qsm = sba("qsm", [128, 64], F32)
            qscale = sba("qscale", [128, 32], BF16)

            jg = rpt[:, 0:64]
            invf = rpt[:, 64:128]
            fq = rpt[:, 128:192]
            tq = rpt[:, 192:256]
            tq2 = rpt[:, 256:320]
            tr = rpt[:, 320:384]
            tr2 = rpt[:, 384:448]
            pc = rpt[:, 448:449]
            P.op(POOL, lambda eo: eo.iota(jg, pattern=[[1, 64]], base=0,
                                          channel_multiplier=0,
                                          allow_small_or_imprecise_dtypes=True),
                 writes=[r_tmp])
            P.op(ACT, lambda eo: eo.activation(invf, jg, AF.Exp,
                                               scale=-LN1E4 / 64.0),
                 reads=[r_tmp], writes=[r_tmp])
            for kt in range(NKT):
                P.op(POOL, lambda eo, kt=kt: eo.iota(
                    pc, pattern=[[0, 1]], base=kt * 128, channel_multiplier=1,
                    allow_small_or_imprecise_dtypes=True),
                    reads=[r_tmp], writes=[r_tmp])
                P.op(DVE, lambda eo: eo.tensor_scalar(fq, invf, pc, None,
                                                      ALU.mult),
                     reads=[r_tmp], writes=[r_tmp])
                for dstap, extra in ((sinb, 0.0), (cosb, 0.25)):
                    if extra:
                        P.op(DVE, lambda eo: eo.tensor_scalar(
                            tq, fq, INV_2PI, 0.25, ALU.mult, ALU.add),
                            reads=[r_tmp], writes=[r_tmp])
                        P.op(DVE, lambda eo: eo.tensor_scalar(
                            tq, tq, MAGIC, None, ALU.add),
                            reads=[r_tmp], writes=[r_tmp])
                    else:
                        P.op(DVE, lambda eo: eo.tensor_scalar(
                            tq, fq, INV_2PI, MAGIC, ALU.mult, ALU.add),
                            reads=[r_tmp], writes=[r_tmp])
                    P.op(DVE, lambda eo: eo.tensor_scalar(
                        tq2, tq, MAGIC, None, ALU.subtract),
                        reads=[r_tmp], writes=[r_tmp])
                    P.op(DVE, lambda eo: eo.scalar_tensor_tensor(
                        tr, tq2, -TWO_PI_HI, fq, ALU.mult, ALU.add),
                        reads=[r_tmp], writes=[r_tmp])
                    P.op(DVE, lambda eo: eo.scalar_tensor_tensor(
                        tr2, tq2, -TWO_PI_LO, tr, ALU.mult, ALU.add),
                        reads=[r_tmp], writes=[r_tmp])
                    if extra:
                        P.op(DVE, lambda eo: eo.tensor_scalar(
                            tr2, tr2, -4.7123885, 1.5707962, ALU.max, ALU.min),
                            reads=[r_tmp], writes=[r_tmp])
                        P.op(ACT, lambda eo, kt=kt, dstap=dstap:
                             eo.activation(dstap[:, kt * 64:(kt + 1) * 64],
                                           tr2, AF.Sin, bias=cns[:, 0:1]),
                             reads=[r_tmp, r_cns], writes=[r_cos])
                    else:
                        P.op(DVE, lambda eo: eo.tensor_scalar(
                            tr2, tr2, -3.1415923, 3.1415923, ALU.max, ALU.min),
                            reads=[r_tmp], writes=[r_tmp])
                        P.op(ACT, lambda eo, kt=kt, dstap=dstap:
                             eo.activation(dstap[:, kt * 64:(kt + 1) * 64],
                                           tr2, AF.Sin),
                             reads=[r_tmp], writes=[r_sin])

            if "d_cos" in dbg_out:
                P.op(SP, lambda eo: eo.dma_start(dbg_out["d_cos"], cosb[:]),
                     reads=[r_cos], dma=True)
                P.op(SP, lambda eo: eo.dma_start(dbg_out["d_sin"], sinb[:]),
                     reads=[r_sin], dma=True)

            quant_jobs = []
            for i in range(2 * HFF // 128):
                for h in range(2):
                    quant_jobs.append(("gu", i, h))
            for i in range(D // 128):
                for k in range(HFF // 1024):
                    quant_jobs.append(("wp", i, k))
            qctr = [0]
            QW = 1024
            NG = QW // 64

            def emit_quant(job):
                kind, i, c = job
                ph = qctr[0] & 1
                qctr[0] += 1
                r_q = r_qt2[ph]
                qwbv = qwb[:, ph * QW:(ph + 1) * QW]
                qawv = qaw[:, ph * QW:(ph + 1) * QW]
                qmv = qm[:, ph * QW:(ph + 1) * QW]
                qsv = qs[:, ph * QW:(ph + 1) * QW]
                gsum = qsm[:, ph * 32:ph * 32 + NG]
                thr = qsm[:, ph * 32 + 16:ph * 32 + 16 + NG]
                qscalev = qscale[:, ph * 16:ph * 16 + NG]
                if kind == "gu":
                    src = wgu[i * 128:(i + 1) * 128, c * QW:(c + 1) * QW]
                    dst = wgu_h[i * 128:(i + 1) * 128, c * QW:(c + 1) * QW]
                    r_dst = r_wguh
                else:
                    src = wp[i * 128:(i + 1) * 128, c * QW:(c + 1) * QW]
                    dst = wp_h[i * 128:(i + 1) * 128, c * QW:(c + 1) * QW]
                    r_dst = r_wph
                P.op(POOL, lambda eo, src=src, qwbv=qwbv:
                     eo.dma_start(qwbv, src),
                     writes=[r_q], dma=True, tag="qload", dslot=4 + ph)
                P.op(ACT, lambda eo, qawv=qawv, qwbv=qwbv:
                     eo.activation(qawv, qwbv, AF.Abs),
                     reads=[r_q], writes=[r_q])
                P.op(ACT, lambda eo, qsv=qsv, qwbv=qwbv:
                     eo.activation(qsv, qwbv, AF.Sign),
                     reads=[r_q], writes=[r_q])
                P.op(DVE, lambda eo, gsum=gsum, qawv=qawv: eo.tensor_reduce(
                    gsum, qawv.rearrange("p (g e) -> p g e", e=64),
                    AX.X, ALU.add),
                    reads=[r_q], writes=[r_q])
                P.op(DVE, lambda eo, qscalev=qscalev, gsum=gsum:
                     eo.tensor_scalar(
                    qscalev, gsum, 1.0 / 64.0, 1e-8, ALU.mult, ALU.max),
                    reads=[r_q], writes=[r_q])
                P.op(DVE, lambda eo, thr=thr, qscalev=qscalev:
                     eo.tensor_scalar(
                    thr, qscalev, QTHR, None, ALU.mult),
                    reads=[r_q], writes=[r_q])
                P.op(DVE, lambda eo, qmv=qmv, qawv=qawv, thr=thr:
                     eo.tensor_tensor(
                    qmv.rearrange("p (g e) -> p g e", e=64),
                    qawv.rearrange("p (g e) -> p g e", e=64),
                    thr.rearrange("p (g e) -> p g e", e=1)
                       .broadcast_to([128, NG, 64]),
                    ALU.is_gt),
                    reads=[r_q], writes=[r_q])
                P.op(DVE, lambda eo, qawv=qawv, qmv=qmv, qsv=qsv:
                     eo.tensor_tensor(qawv, qmv, qsv, ALU.mult),
                     reads=[r_q], writes=[r_q])
                P.op(DVE, lambda eo, qmv=qmv, qawv=qawv, qscalev=qscalev:
                     eo.tensor_tensor(
                    qmv.rearrange("p (g e) -> p g e", e=64),
                    qawv.rearrange("p (g e) -> p g e", e=64),
                    qscalev.rearrange("p (g e) -> p g e", e=1)
                        .broadcast_to([128, NG, 64]),
                    ALU.mult),
                    reads=[r_q], writes=[r_q])
                P.op(SP, lambda eo, dst=dst, qmv=qmv: eo.dma_start(dst, qmv),
                     reads=[r_q], writes=[r_dst], dma=True, tag="qstore",
                     dslot=5, join=True)

            def pump_quant(n=1):
                for _ in range(2 * n):
                    if quant_jobs:
                        emit_quant(quant_jobs.pop(0))

            def rope6(src_f32_1024, src_deps, nh8, kti, dst_bf_1024, dst_res):
                e = 64
                sv = src_f32_1024.rearrange("p (h q) -> p h q", q=128)
                k1 = sv[:, :, 0:e]
                k2 = sv[:, :, e:128]
                cb = cosb[:, kti * 64:(kti + 1) * 64].unsqueeze(1) \
                    .broadcast_to([128, nh8, e])
                sbn = sinb[:, kti * 64:(kti + 1) * 64].unsqueeze(1) \
                    .broadcast_to([128, nh8, e])
                t1 = rpt[:, 0:nh8 * e].rearrange("p (h q) -> p h q", q=e)
                t2 = rpt[:, 512:512 + nh8 * e].rearrange("p (h q) -> p h q", q=e)
                dv = dst_bf_1024.rearrange("p (h q) -> p h q", q=128)
                o1 = dv[:, :, 0:e]
                o2 = dv[:, :, e:128]
                rd = list(src_deps) + [r_cos, r_sin]
                P.op(DVE, lambda eo: eo.tensor_tensor(t1, k1, cb, ALU.mult),
                     reads=rd, writes=[r_tmp2])
                P.op(DVE, lambda eo: eo.tensor_tensor(t2, k2, sbn, ALU.mult),
                     reads=rd, writes=[r_tmp2])
                P.op(DVE, lambda eo: eo.tensor_tensor(o1, t1, t2, ALU.subtract),
                     reads=[r_tmp2], writes=[dst_res])
                P.op(DVE, lambda eo: eo.tensor_tensor(t1, k2, cb, ALU.mult),
                     reads=rd, writes=[r_tmp2])
                P.op(DVE, lambda eo: eo.tensor_tensor(t2, k1, sbn, ALU.mult),
                     reads=rd, writes=[r_tmp2])
                P.op(DVE, lambda eo: eo.tensor_tensor(o2, t1, t2, ALU.add),
                     reads=[r_tmp2], writes=[dst_res])

            r_tmp2 = r_tmp

            for seq in range(B):
                for g in range(4):
                    for t in range(4):
                        kt = g * 4 + t
                        xs = t % 2
                        xap = xbuf[:, xs * D:(xs + 1) * D]
                        aap = abf[:, xs * D:(xs + 1) * D]
                        P.op(SP, lambda eo, seq=seq, kt=kt, xap=xap:
                             eo.dma_start(xap,
                                          x_in[seq, kt * 128:kt * 128 + 128, :]),
                             writes=[r_xb[xs]], dma=True, dslot=xs)
                        rs, nb, rstat = ln_stats(xap, [r_xb[xs]], (t % 2) * 8)
                        ln_apply(xap, [r_xb[xs]], aap, r_ab[xs], rs, nb, rstat)
                        transpose16(aap, [r_ab[xs]],
                                    afm[:, t * D:(t + 1) * D], r_afm[t], t)
                    for dc in range(16):
                        ws = dc % 3
                        wap = wring[:, ws * D:ws * D + 1024]
                        P.op(SP, lambda eo, dc=dc, wap=wap: eo.dma_start(
                            wap[:, 0:512],
                            wk_bf[0:DKV, dc * 128:(dc + 1) * 128],
                            transpose=True),
                            reads=[r_wkbf], writes=[r_wr[ws]], dma=True,
                            dslot=2 + ws)
                        P.op(SP, lambda eo, dc=dc, wap=wap: eo.dma_start(
                            wap[:, 512:1024],
                            wv_bf[0:DKV, dc * 128:(dc + 1) * 128],
                            transpose=True),
                            reads=[r_wvbf], writes=[r_wr[ws]], dma=True,
                            dslot=2 + ws, join=True)
                        for t in range(4):
                            ps_f, _, banks = pslot(t)
                            for n in range(2):
                                P.op(TE, lambda eo, t=t, dc=dc, n=n, wap=wap,
                                     ps_f=ps_f: eo.matmul(
                                         ps_f[:, n * 512:(n + 1) * 512],
                                         afm[:, t * D + dc * 128:
                                             t * D + (dc + 1) * 128],
                                         wap[:, n * 512:(n + 1) * 512],
                                         start=(dc == 0), stop=(dc == 15)),
                                     reads=[r_afm[t], r_wr[ws]], writes=banks)
                    for t in range(4):
                        kt = g * 4 + t
                        ps_f, ps_b, banks = pslot(t)
                        ks = 0
                        kap = krop[:, 0:DKV]
                        rope6(ps_f[:, 0:512], banks, NKV, kt, kap, r_kr[ks])
                        P.op(DVE, lambda eo, kt=kt, ps_f=ps_f: eo.tensor_copy(
                            vtm[:, kt * DKV:(kt + 1) * DKV],
                            ps_f[:, 512:1024]),
                            reads=banks, writes=[r_vtm])
                        for c in range(4):
                            P.op(TE, lambda eo, c=c, kap=kap, ps_b=ps_b:
                                 eo.transpose(ps_b[:, c * 128:(c + 1) * 128],
                                              kap[:, c * 128:(c + 1) * 128],
                                              ident[:]),
                                 reads=[r_kr[ks], r_ident], writes=banks)
                        P.op(DVE, lambda eo, kt=kt, ps_b=ps_b: eo.tensor_copy(
                            kfm[:].rearrange("p (h k) -> p h k", k=S)
                                  [:, :, kt * 128:kt * 128 + 128],
                            ps_b[:, 0:512].rearrange("p (h k) -> p h k", k=128)),
                            reads=banks, writes=[r_kfm])
                        pump_quant(1)

                if seq == 0 and "d_kfm" in dbg_out:
                    P.op(POOL, lambda eo: eo.dma_start(dbg_out["d_kfm"], kfm[:]),
                         reads=[r_kfm], dma=True)
                    P.op(POOL, lambda eo: eo.dma_start(dbg_out["d_vtm"], vtm[:]),
                         reads=[r_vtm], dma=True)

                for qtile in range(S // QT):
                    for g2 in range(QT // 512):
                        for t in range(4):
                            tsub = g2 * 4 + t
                            row0 = qtile * QT + tsub * 128
                            xs = t % 2
                            xap = xbuf[:, xs * D:(xs + 1) * D]
                            aap = abf[:, xs * D:(xs + 1) * D]
                            P.op(SP, lambda eo, seq=seq, row0=row0, xap=xap:
                                 eo.dma_start(xap,
                                              x_in[seq, row0:row0 + 128, :]),
                                 writes=[r_xb[xs]], dma=True, dslot=xs)
                            rs, nb, rstat = ln_stats(xap, [r_xb[xs]],
                                                     (t % 2) * 8)
                            ln_apply(xap, [r_xb[xs]], aap, r_ab[xs],
                                     rs, nb, rstat)
                            transpose16(aap, [r_ab[xs]],
                                        afm[:, t * D:(t + 1) * D],
                                        r_afm[t], t)
                        for half in range(2):
                            for dc in range(16):
                                ws = dc % 3
                                wap = wring[:, ws * D:ws * D + 1024]
                                P.op(SP, lambda eo, dc=dc, half=half, wap=wap:
                                     eo.dma_start(
                                         wap,
                                         wq_bf[half * 1024:(half + 1) * 1024,
                                               dc * 128:(dc + 1) * 128],
                                         transpose=True),
                                     reads=[r_wqbf], writes=[r_wr[ws]],
                                     dma=True, dslot=2 + ws)
                                for t in range(4):
                                    ps_f, _, banks = pslot(t)
                                    for n in range(2):
                                        P.op(TE, lambda eo, t=t, dc=dc, n=n,
                                             wap=wap, ps_f=ps_f: eo.matmul(
                                                 ps_f[:, n * 512:(n + 1) * 512],
                                                 afm[:, t * D + dc * 128:
                                                     t * D + (dc + 1) * 128],
                                                 wap[:, n * 512:(n + 1) * 512],
                                                 start=(dc == 0),
                                                 stop=(dc == 15)),
                                             reads=[r_afm[t], r_wr[ws]],
                                             writes=banks)
                            for t in range(4):
                                tsub = g2 * 4 + t
                                ktq = qtile * (QT // 128) + tsub
                                ps_f, _, banks = pslot(t)
                                rope6(ps_f[:, 0:1024], banks, 8, ktq,
                                      qbf[:, t * D + half * 1024:
                                          t * D + (half + 1) * 1024],
                                      r_qb[t])
                        for t in range(4):
                            tsub = g2 * 4 + t
                            _, ps_b, banks = pslot(t)
                            for c in range(16):
                                P.op(TE, lambda eo, c=c, t=t, ps_b=ps_b:
                                     eo.transpose(
                                         ps_b[:, c * 128:(c + 1) * 128],
                                         qbf[:, t * D + c * 128:
                                             t * D + (c + 1) * 128],
                                         ident[:]),
                                     reads=[r_qb[t], r_ident], writes=banks)
                            P.op(DVE, lambda eo, tsub=tsub, ps_b=ps_b:
                                 eo.tensor_copy(
                                     qfm[:].rearrange("p (h q) -> p h q", q=QT)
                                           [:, :, tsub * 128:tsub * 128 + 128],
                                     ps_b[:, 0:2048].rearrange(
                                         "p (h q) -> p h q", q=128)),
                                 reads=banks, writes=[r_qfm])
                        pump_quant(2)

                    if seq == 0 and qtile == 0 and "d_qfm" in dbg_out:
                        P.op(POOL, lambda eo: eo.dma_start(dbg_out["d_qfm"],
                                                           qfm[:]),
                             reads=[r_qfm], dma=True)

                    for kvh in range(NKV):
                        for hh in range(4):
                            h = kvh * 4 + hh
                            oS = [pslot(0), pslot(1)]
                            oO_f, _, oO_banks = pslot(2)
                            dn_f, _, dn_banks = pslot(3)
                            for kt in range(NKT):
                                s_f, _, s_banks = oS[kt % 2]
                                es = kt % 2
                                eap = expb[:, es * QT:(es + 1) * QT]
                                for n in range(QT // 512):
                                    P.op(TE, lambda eo, kvh=kvh, h=h, kt=kt,
                                         n=n, s_f=s_f: eo.matmul(
                                             s_f[:, n * 512:(n + 1) * 512],
                                             kfm[:, kvh * S + kt * 128:
                                                 kvh * S + (kt + 1) * 128],
                                             qfm[:, h * QT + n * 512:
                                                 h * QT + (n + 1) * 512]),
                                         reads=[r_kfm, r_qfm], writes=s_banks)
                                P.op(ACT, lambda eo, s_f=s_f, eap=eap:
                                     eo.activation(eap, s_f[:, 0:QT], AF.Exp,
                                                   scale=ISQ),
                                     reads=s_banks, writes=[r_ex[es]])
                                if kt == 0:
                                    P.op(POOL, lambda eo, eap=eap:
                                         eo.tensor_copy(dacc[:], eap),
                                         reads=[r_ex[es]], writes=[r_da])
                                else:
                                    P.op(POOL, lambda eo, eap=eap:
                                         eo.tensor_tensor(dacc[:], dacc[:],
                                                          eap, ALU.add),
                                         reads=[r_ex[es], r_da],
                                         writes=[r_da])
                                for n in range(QT // 512):
                                    P.op(TE, lambda eo, kvh=kvh, kt=kt, n=n,
                                         eap=eap, oO_f=oO_f: eo.matmul(
                                             oO_f[:, n * 512:(n + 1) * 512],
                                             vtm[:, kt * DKV + kvh * 128:
                                                 kt * DKV + (kvh + 1) * 128],
                                             eap[:, n * 512:(n + 1) * 512],
                                             start=(kt == 0),
                                             stop=(kt == NKT - 1)),
                                         reads=[r_vtm, r_ex[es]],
                                         writes=oO_banks)
                            for n in range(QT // 512):
                                P.op(TE, lambda eo, n=n, dn_f=dn_f: eo.matmul(
                                    dn_f[0:1, n * 512:(n + 1) * 512],
                                    ones_col[:], dacc[:, n * 512:(n + 1) * 512]),
                                    reads=[r_da, r_ones], writes=dn_banks)
                            P.op(DVE, lambda eo, dn_f=dn_f: eo.tensor_copy(
                                dsb[0:1, 0:QT], dn_f[0:1, 0:QT]),
                                reads=dn_banks, writes=[r_ds])
                            P.op(DVE, lambda eo: eo.reciprocal(
                                rden[0:1, 0:QT], dsb[0:1, 0:QT]),
                                reads=[r_ds], writes=[r_rd])
                            for n in range(QT // 512):
                                P.op(TE, lambda eo, n=n, dn_f=dn_f: eo.matmul(
                                    dn_f[:, n * 512:(n + 1) * 512],
                                    ones_row[:],
                                    rden[0:1, n * 512:(n + 1) * 512]),
                                    reads=[r_rd, r_ones], writes=dn_banks)
                            P.op(DVE, lambda eo, dn_f=dn_f: eo.tensor_copy(
                                rden[:], dn_f[:, 0:QT]),
                                reads=dn_banks, writes=[r_rd])
                            P.op(DVE, lambda eo, h=h, oO_f=oO_f:
                                 eo.tensor_tensor(
                                     ofm[:, h * QT:(h + 1) * QT],
                                     oO_f[:, 0:QT], rden[:], ALU.mult),
                                 reads=oO_banks + [r_rd], writes=[r_ofm])
                            pump_quant(2)

                    if seq == 0 and qtile == 0 and "d_ofm" in dbg_out:
                        P.op(POOL, lambda eo: eo.dma_start(dbg_out["d_ofm"],
                                                           ofm[:]),
                             reads=[r_ofm], dma=True)

                    for g2 in range(QT // 512):
                        for half in range(2):
                            for c in range(16):
                                ws = c % 3
                                wap = wring[:, ws * D:ws * D + 1024]
                                P.op(SP, lambda eo, c=c, half=half, wap=wap:
                                     eo.dma_start(
                                         wap,
                                         wo_bf[half * 1024:(half + 1) * 1024,
                                               c * 128:(c + 1) * 128],
                                         transpose=True),
                                     reads=[r_wobf], writes=[r_wr[ws]],
                                     dma=True, dslot=2 + ws)
                                for t in range(4):
                                    tsub = g2 * 4 + t
                                    ps_f, _, banks = pslot(t)
                                    for n in range(2):
                                        P.op(TE, lambda eo, c=c, tsub=tsub,
                                             n=n, wap=wap, ps_f=ps_f:
                                             eo.matmul(
                                                 ps_f[:, n * 512:(n + 1) * 512],
                                                 ofm[:, c * QT + tsub * 128:
                                                     c * QT + tsub * 128 + 128],
                                                 wap[:, n * 512:(n + 1) * 512],
                                                 start=(c == 0),
                                                 stop=(c == 15)),
                                             reads=[r_ofm, r_wr[ws]],
                                             writes=banks)
                            for t in range(4):
                                ps_f, _, banks = pslot(t)
                                col = 16 + t * 16
                                su = stt[:, col + 4 * half:col + 4 * half + 1]
                                sq = stt[:, col + 4 * half + 1:
                                         col + 4 * half + 2]
                                P.op(DVE, lambda eo, ps_f=ps_f, su=su:
                                     eo.tensor_reduce(su, ps_f[:, 0:1024],
                                                      AX.X, ALU.add),
                                     reads=banks, writes=[r_st[8 + t]])
                                if half == 0:
                                    P.op(DVE, lambda eo, t=t, ps_f=ps_f:
                                         eo.tensor_copy(
                                             fstage[:, t * 1024:
                                                    (t + 1) * 1024],
                                             ps_f[:, 0:1024]),
                                         reads=banks, writes=[r_fs[t]])
                                extra = [r_fs[t]] if half == 0 else []
                                P.op(ACT, lambda eo, ps_f=ps_f, sq=sq:
                                     eo.activation(sqscr[:, 0:1024],
                                                   ps_f[:, 0:1024], AF.Square,
                                                   accum_out=sq),
                                     reads=banks + extra,
                                     writes=[r_st[8 + t], r_sqscr])
                        for t in range(4):
                            tsub = g2 * 4 + t
                            row0 = qtile * QT + tsub * 128
                            grow = seq * S + row0
                            ps_f, _, banks = pslot(t)
                            col = 16 + t * 16
                            su0 = stt[:, col + 0:col + 1]
                            sq0 = stt[:, col + 1:col + 2]
                            su1 = stt[:, col + 4:col + 5]
                            sq1 = stt[:, col + 5:col + 6]
                            m = stt[:, col + 8:col + 9]
                            ms = stt[:, col + 9:col + 10]
                            v = stt[:, col + 10:col + 11]
                            v2 = stt[:, col + 11:col + 12]
                            rs = stt[:, col + 12:col + 13]
                            nb = stt[:, col + 13:col + 14]
                            rstat = r_st[8 + t]
                            P.op(DVE, lambda eo, su0=su0, su1=su1, m=m:
                                 eo.tensor_tensor(m, su0, su1, ALU.add),
                                 reads=[rstat], writes=[rstat])
                            P.op(DVE, lambda eo, m=m: eo.tensor_scalar(
                                m, m, 1.0 / D, None, ALU.mult),
                                reads=[rstat], writes=[rstat])
                            P.op(DVE, lambda eo, sq0=sq0, sq1=sq1, v=v:
                                 eo.tensor_tensor(v, sq0, sq1, ALU.add),
                                 reads=[rstat], writes=[rstat])
                            P.op(DVE, lambda eo, v=v: eo.tensor_scalar(
                                v, v, 1.0 / D, None, ALU.mult),
                                reads=[rstat], writes=[rstat])
                            P.op(DVE, lambda eo, m=m, ms=ms: eo.tensor_scalar(
                                ms, m, m, None, ALU.mult),
                                reads=[rstat], writes=[rstat])
                            P.op(DVE, lambda eo, v=v, ms=ms, v2=v2:
                                 eo.tensor_tensor(v2, v, ms, ALU.subtract),
                                 reads=[rstat], writes=[rstat])
                            P.op(ACT, lambda eo, v2=v2, ms=ms: eo.activation(
                                ms, v2, AF.Ln, bias=cns[:, 1:2]),
                                reads=[rstat, r_cns], writes=[rstat])
                            P.op(ACT, lambda eo, ms=ms, rs=rs: eo.activation(
                                rs, ms, AF.Exp, scale=-0.5),
                                reads=[rstat], writes=[rstat])
                            P.op(DVE, lambda eo, m=m, rs=rs, nb=nb:
                                 eo.tensor_scalar(nb, m, rs, -1.0,
                                                  ALU.mult, ALU.mult),
                                 reads=[rstat], writes=[rstat])
                            xs = t % 2
                            xap = xbuf[:, xs * D:(xs + 1) * D]
                            P.op(SP, lambda eo, seq=seq, row0=row0, xap=xap:
                                 eo.dma_start(xap,
                                              x_in[seq, row0:row0 + 128, :]),
                                 writes=[r_xb[xs]], dma=True, dslot=xs)
                            P.op(ACT, lambda eo, t=t, rs=rs, nb=nb:
                                 eo.activation(x1st[:, 0:1024],
                                               fstage[:, t * 1024:
                                                      (t + 1) * 1024],
                                               AF.Identity, bias=nb, scale=rs),
                                 reads=[r_fs[t], rstat], writes=[r_x1st])
                            P.op(ACT, lambda eo, ps_f=ps_f, rs=rs, nb=nb:
                                 eo.activation(x1st[:, 1024:2048],
                                               ps_f[:, 0:1024],
                                               AF.Identity, bias=nb, scale=rs),
                                 reads=banks + [rstat], writes=[r_x1st])
                            P.op(DVE, lambda eo, xap=xap: eo.tensor_tensor(
                                x1o[:], x1st[:], xap, ALU.add),
                                reads=[r_x1st, r_xb[xs]], writes=[r_x1o])
                            P.op(SP, lambda eo, grow=grow: eo.dma_start(
                                x1scr[grow:grow + 128, :], x1o[:]),
                                reads=[r_x1o], writes=[r_x1[grow // 128]],
                                dma=True, dslot=6)
                        pump_quant(2)

            while quant_jobs:
                emit_quant(quant_jobs.pop(0))

            if "d_x1" in dbg_out:
                P.op(SP, lambda eo: eo.dma_start(dbg_out["d_x1"], x1scr),
                     reads=r_x1, dma=True)
            if "d_wgu" in dbg_out:
                P.op(SP, lambda eo: eo.dma_start(dbg_out["d_wgu"], wgu_h),
                     reads=[r_wguh], dma=True)
                P.op(SP, lambda eo: eo.dma_start(dbg_out["d_wp"], wp_h),
                     reads=[r_wph], dma=True)

        P.barrier("attn->ffn")

        with contextlib.ExitStack() as sf:
            def sbf(name, shape, dt):
                return sf.enter_context(nc.sbuf_tensor(name, shape, dt))

            gur = sbf("gur", [128, 3 * 16 * 512], BF16)
            wpr = sbf("wpr", [128, 3 * D], BF16)
            hfm = sbf("hfm", [128, 64 * TFF], BF16)
            ffm = sbf("ffm", [128, 16 * TFF], BF16)
            ftmp = sbf("ftmp", [128, 512], F32)
            ffsb = sbf("ffsb", [128, D], F32)
            rsums = sbf("rsums", [128, 8], F32)
            dsb_ffn = sbf("dsb_ffn", [1, 512], F32)

            for Ti in range(TOK // TFF):
                trow = Ti * TFF
                for ts in range(4):
                    grow = trow + ts * 128
                    xs = ts % 2
                    xap = xbuf[:, xs * D:(xs + 1) * D]
                    aap = abf[:, xs * D:(xs + 1) * D]
                    P.op(SP, lambda eo, grow=grow, xap=xap: eo.dma_start(
                        xap, x1scr[grow:grow + 128, :]),
                        reads=[r_x1[grow // 128]], writes=[r_xb[xs]],
                        dma=True, dslot=xs)
                    rs, nb, rstat = ln_stats(xap, [r_xb[xs]], (ts % 2) * 8)
                    ln_apply(xap, [r_xb[xs]], aap, r_ab[xs], rs, nb, rstat)
                    ps_f, ps_b, banks = pslot(2 + ts % 2)
                    for c in range(16):
                        P.op(TE, lambda eo, c=c, aap=aap, ps_b=ps_b:
                             eo.transpose(ps_b[:, c * 128:(c + 1) * 128],
                                          aap[:, c * 128:(c + 1) * 128],
                                          ident[:]),
                             reads=[r_ab[xs], r_ident], writes=banks)
                    P.op(DVE, lambda eo, ts=ts, ps_b=ps_b: eo.tensor_copy(
                        ffm[:].rearrange("p (dc t) -> p dc t", t=TFF)
                              [:, :, ts * 128:ts * 128 + 128],
                        ps_b[:, 0:2048].rearrange("p (dc t) -> p dc t", t=128)),
                        reads=banks, writes=[r_ffm])

                sq_f, sq_banks = pbank(4)
                nsl = 0
                for s in range(16):
                    for gu_i in range(2):
                        sl = nsl % 3
                        nsl += 1
                        hrow0 = gu_i * HFF + s * 512
                        base = sl * 16 * 512
                        for dc in range(16):
                            P.op(SP, lambda eo, hrow0=hrow0, dc=dc, base=base:
                                 eo.dma_start(
                                     gur[:, base + dc * 512:
                                         base + (dc + 1) * 512],
                                     wgu_h[hrow0:hrow0 + 512,
                                           dc * 128:(dc + 1) * 128],
                                     transpose=True),
                                 reads=[r_wguh], writes=[r_gur[sl]], dma=True,
                                 dslot=2 + sl, join=(dc > 0))
                        if gu_i == 0:
                            gslot = sl
                        else:
                            uslot = sl
                    gbase = gslot * 16 * 512
                    ubase = uslot * 16 * 512
                    for hsub in range(4):
                        hc = s * 4 + hsub
                        pg_f, pg_banks = pbank((hsub % 2) * 2)
                        pu_f, pu_banks = pbank((hsub % 2) * 2 + 1)
                        for dc in range(16):
                            P.op(TE, lambda eo, dc=dc, hsub=hsub, gbase=gbase,
                                 pg_f=pg_f: eo.matmul(
                                     pg_f,
                                     gur[:, gbase + dc * 512 + hsub * 128:
                                         gbase + dc * 512 + hsub * 128 + 128],
                                     ffm[:, dc * TFF:(dc + 1) * TFF],
                                     start=(dc == 0), stop=(dc == 15)),
                                 reads=[r_gur[gslot], r_ffm], writes=pg_banks)
                            P.op(TE, lambda eo, dc=dc, hsub=hsub, ubase=ubase,
                                 pu_f=pu_f: eo.matmul(
                                     pu_f,
                                     gur[:, ubase + dc * 512 + hsub * 128:
                                         ubase + dc * 512 + hsub * 128 + 128],
                                     ffm[:, dc * TFF:(dc + 1) * TFF],
                                     start=(dc == 0), stop=(dc == 15)),
                                 reads=[r_gur[uslot], r_ffm], writes=pu_banks)
                        hap = hfm[:, hc * TFF:(hc + 1) * TFF]
                        P.op(ACT, lambda eo, pg_f=pg_f: eo.activation(
                            ftmp[:], pg_f, AF.Silu),
                            reads=pg_banks, writes=[r_ftmp])
                        P.op(DVE, lambda eo, hap=hap, pu_f=pu_f:
                             eo.tensor_tensor(hap, ftmp[:], pu_f, ALU.mult),
                             reads=[r_ftmp] + pu_banks, writes=[r_hfm])
                        sqs = hc % 2
                        P.op(ACT, lambda eo, hap=hap, sqs=sqs: eo.activation(
                            sqscr[:, sqs * 512:(sqs + 1) * 512], hap,
                            AF.Square),
                            reads=[r_hfm], writes=[r_sqb[sqs], r_sqscr])
                        P.op(TE, lambda eo, sqs=sqs, hc=hc, sq_f=sq_f:
                             eo.matmul(sq_f[0:1, 0:512], ones_bcol[:],
                                       sqscr[:, sqs * 512:(sqs + 1) * 512],
                                       start=(hc == 0), stop=(hc == 63)),
                             reads=[r_sqb[sqs], r_ones], writes=sq_banks)

                P.op(DVE, lambda eo, sq_f=sq_f: eo.tensor_copy(
                    dsb_ffn[0:1, 0:512], sq_f[0:1, 0:512]),
                    reads=sq_banks, writes=[r_rst])
                tr_f, tr_b, tr_banks = pslot(3)
                for c in range(4):
                    P.op(TE, lambda eo, c=c, tr_f=tr_f: eo.transpose(
                        tr_f[:, c:c + 1],
                        dsb_ffn[0:1, c * 128:(c + 1) * 128],
                        identf[0:1, 0:1]),
                        reads=[r_rst, r_identf], writes=tr_banks)
                P.op(DVE, lambda eo, tr_f=tr_f: eo.tensor_copy(
                    rsums[:, 0:4], tr_f[:, 0:4]),
                    reads=tr_banks, writes=[r_rst])
                P.op(ACT, lambda eo: eo.activation(
                    rsums[:, 4:8], rsums[:, 0:4], AF.Ln,
                    scale=1.0 / HFF, bias=cns[:, 2:3]),
                    reads=[r_rst, r_cns], writes=[r_rst])
                P.op(ACT, lambda eo: eo.activation(
                    rsums[:, 0:4], rsums[:, 4:8], AF.Exp, scale=-0.5),
                    reads=[r_rst], writes=[r_rst])

                if Ti == 0 and "d_hfm" in dbg_out:
                    P.op(POOL, lambda eo: eo.dma_start(dbg_out["d_hfm"],
                                                       hfm[:]),
                         reads=[r_hfm], dma=True)

                for pr in range(2):
                    pboth = [(pA[:, 0:2048], [bankres[k] for k in range(4)]),
                             (pB[:, 0:2048], [bankres[4 + k] for k in range(4)])]
                    for hc in range(64):
                        ws = hc % 3
                        wap = wpr[:, ws * D:(ws + 1) * D]
                        P.op(SP, lambda eo, hc=hc, wap=wap: eo.dma_start(
                            wap, wp_h[0:D, hc * 128:(hc + 1) * 128],
                            transpose=True),
                            reads=[r_wph], writes=[r_wpr[ws]], dma=True,
                            dslot=8 + ws)
                        for pi in range(2):
                            ts = pr * 2 + pi
                            pp_f, pp_banks = pboth[pi]
                            for n in range(4):
                                P.op(TE, lambda eo, hc=hc, ts=ts, n=n,
                                     wap=wap, pp_f=pp_f: eo.matmul(
                                         pp_f[:, n * 512:(n + 1) * 512],
                                         hfm[:, hc * TFF + ts * 128:
                                             hc * TFF + ts * 128 + 128],
                                         wap[:, n * 512:(n + 1) * 512],
                                         start=(hc == 0), stop=(hc == 63)),
                                     reads=[r_hfm, r_wpr[ws]],
                                     writes=pp_banks)
                    for pi in range(2):
                        ts = pr * 2 + pi
                        grow = trow + ts * 128
                        b0, s0 = divmod(grow, S)
                        pp_f, pp_banks = pboth[pi]
                        rsap = rsums[:, ts:ts + 1]
                        P.op(DVE, lambda eo, pp_f=pp_f, rsap=rsap:
                             eo.tensor_scalar(ffsb[:], pp_f, rsap, None,
                                              ALU.mult),
                             reads=pp_banks + [r_rst], writes=[r_ffsb])
                        rs, nb, rstat = ln_stats(ffsb[:], [r_ffsb], 64)
                        xs = ts % 2
                        xap = xbuf[:, xs * D:(xs + 1) * D]
                        P.op(SP, lambda eo, grow=grow, xap=xap: eo.dma_start(
                            xap, x1scr[grow:grow + 128, :]),
                            reads=[r_x1[grow // 128]], writes=[r_xb[xs]],
                            dma=True, dslot=xs)
                        P.op(ACT, lambda eo, rs=rs, nb=nb: eo.activation(
                            x1st[:], ffsb[:], AF.Identity, bias=nb, scale=rs),
                            reads=[r_ffsb, rstat], writes=[r_x1st])
                        P.op(DVE, lambda eo, xap=xap: eo.tensor_tensor(
                            ffsb[:], x1st[:], xap, ALU.add),
                            reads=[r_x1st, r_xb[xs]], writes=[r_ffsb])
                        P.op(SP, lambda eo, b0=b0, s0=s0: eo.dma_start(
                            y_out[b0, s0:s0 + 128, :], ffsb[:]),
                            reads=[r_ffsb], dma=True, dslot=7)

        P.barrier("end")

        with nc.Block() as block:
            P.lower(nc, block, csem, dsem)

    return nc



def kernel(**inputs):
    x = np.ascontiguousarray(inputs["L0_attn_hidden_states"])
    in_common = {
        "wq": np.ascontiguousarray(inputs["L0_attn_Wq"]),
        "wk": np.ascontiguousarray(inputs["L0_attn_Wk"]),
        "wv": np.ascontiguousarray(inputs["L0_attn_Wv"]),
        "wo": np.ascontiguousarray(inputs["L0_attn_Wo"]),
        "wgu": np.ascontiguousarray(inputs["L0_ffn_w_gate_up"]),
        "wp": np.ascontiguousarray(inputs["L0_ffn_w_proj"]),
    }
    nc = build_program()
    in_maps = []
    for c in range(8):
        m = dict(in_common)
        m["x"] = np.ascontiguousarray(x[c * B:(c + 1) * B])
        in_maps.append(m)
    res = run_bass_kernel_spmd(nc, in_maps, core_ids=list(range(8)))
    y = np.concatenate([r["y"] for r in res.results], axis=0)
    return y
